# version 02 (final)
# baseline (speedup 1.0000x reference)
import contextlib
import numpy as np
import concourse.bass as bass
import concourse.mybir as mybir
from concourse.bass_utils import run_bass_kernel_spmd

F32 = mybir.dt.float32
BF16 = mybir.dt.bfloat16
U8 = mybir.dt.uint8
AF = mybir.ActivationFunctionType
OP = mybir.AluOpType
AX = mybir.AxisListType

T = 4096
D = 1024
CH = 512
NCH = T // CH
DT = D // 128
DI = 2048
DIT = DI // 128
NSTATE = 16
EPS = 1e-5

ENG2BODY = {"pe": "tensor", "act": "scalar", "dve": "vector", "pool": "gpsimd", "sp": "sync"}
DMAQ = {"sp": "dsp", "pool": "dpool", "act": "dact"}
SEM_ROTATE = 15000


class Stream:
    __slots__ = ("name", "sems", "idx", "val", "retired")

    def __init__(self, name, sems):
        self.name = name
        self.sems = sems
        self.idx = 0
        self.val = 0
        self.retired = []

    @property
    def sem(self):
        return self.sems[self.idx]

    def bump(self, amt):
        if self.val + amt > SEM_ROTATE:
            self.retired.append((self.sem, self.val))
            self.idx = (self.idx + 1) % len(self.sems)
            assert not any(s is self.sem for s, _ in self.retired), \
                f"stream {self.name}: semaphore pool exhausted"
            self.val = 0
        self.val += amt
        return (self.name, self.sem, self.val)


class Tile:
    __slots__ = ("name", "ap", "writer", "readers", "arena", "off", "nbytes", "_rp", "handle")

    def __init__(self, name, ap, arena=None, off=0, nbytes=0):
        self.name = name
        self.ap = ap
        self.writer = None
        self.readers = []
        self.arena = arena
        self.off = off
        self.nbytes = nbytes

    def free(self):
        if self.arena is not None:
            self.arena._free(self)
            self.arena = None


class Arena:

    def __init__(self, handle, nbytes):
        self.h = handle
        self.free_list = [(0, nbytes, [])]
        self.live = {}

    def alloc(self, ctx, name, dtype, rows, cols, align=64):
        esz = {F32: 4, BF16: 2, U8: 1}[dtype]
        need = -((-cols * esz) // align) * align
        for i, (off, size, hz) in enumerate(self.free_list):
            if size >= need:
                rest = (off + need, size - need, hz)
                self.free_list[i:i + 1] = [rest] if rest[1] > 0 else []
                ap = self.h[0:rows, off:off + cols * esz].bitcast(dtype)
                t = Tile(name, ap, self, off, need)
                self.live[off] = (name, need)
                t.readers = list(hz)
                return t
        raise MemoryError(f"arena: no room for {name} ({need}B/part); free={[(o,s) for o,s,_ in self.free_list]}; live={sorted(self.live.values(), key=lambda x: -x[1])}")

    def _free(self, tile):
        self.live.pop(tile.off, None)
        hz = list(tile.readers)
        if tile.writer:
            hz.append(tile.writer)
        self.free_list.append((tile.off, tile.nbytes, hz))
        self.free_list.sort(key=lambda r: r[0])
        out = []
        for r in self.free_list:
            if out and out[-1][0] + out[-1][1] == r[0]:
                out[-1] = (out[-1][0], out[-1][1] + r[1], out[-1][2] + r[2])
            else:
                out.append(list(r) if isinstance(r, tuple) else r)
        self.free_list = [tuple(r) for r in out]


def _tv(x):
    if isinstance(x, Tile):
        return x, x.ap
    if isinstance(x, tuple):
        return x[0], x[1]
    return None, x


NLANES = {"dsp": 16, "dpool": 12, "dact": 4}


class Ctx:
    def __init__(self, nc, sems, arena_handle, arena_bytes, psum_handles):
        self.nc = nc
        self.arena = Arena(arena_handle, arena_bytes)
        self.streams = {}
        it = iter(sems)
        for name, n in [("pe", 16), ("act", 8), ("dve", 12), ("pool", 4), ("sp", 2)]:
            self.streams[name] = Stream(name, [next(it) for _ in range(n)])
        self.lane_rr = {}
        for q, n in NLANES.items():
            self.lane_rr[q] = 0
            for i in range(n):
                self.streams[f"{q}{i}"] = Stream(f"{q}{i}", [next(it)])
        self.thunks = {e: [] for e in ENG2BODY}
        self.watermark = {}
        self.floor = {}
        self.psum = [Tile(f"psum{i}", h[:, :]) for i, h in enumerate(psum_handles)]
        self._psum_rr = 0
        self.phase_idx = 0


    def _want(self, consumer, dep):
        sname, sem, val = dep
        if sname == consumer and sname == "pe":
            return None
        k = id(sem)
        if self.floor.get(k, -1) >= val:
            return None
        wk = (consumer, k)
        if self.watermark.get(wk, -1) >= val:
            return None
        self.watermark[wk] = val
        return (sem, val)

    def emit(self, eng, build, reads=(), writes=(), dmaq=None, war=()):
        deps = []
        rts, wts = [], []
        for x in war:
            t, _ = _tv(x)
            if t is not None:
                if t.writer:
                    deps.append(t.writer)
                deps.extend(t.readers)
        for x in reads:
            t, _ = _tv(x)
            if t is not None:
                rts.append(t)
                if t.writer:
                    deps.append(t.writer)
        for x in writes:
            t, _ = _tv(x)
            if t is not None:
                wts.append(t)
                if t.writer:
                    deps.append(t.writer)
                deps.extend(t.readers)
        waits = []
        for d in deps:
            w = self._want(eng, d)
            if w is not None:
                waits.append(w)
        if dmaq is not None:
            lane = f"{dmaq}{self.lane_rr[dmaq]}"
            self.lane_rr[dmaq] = (self.lane_rr[dmaq] + 1) % NLANES[dmaq]
            lst = self.streams[lane]
            if lst.val > 0:
                w = self._want(eng, (lane, lst.sem, lst.val))
                if w is not None:
                    waits.append(w)
            rec = lst.bump(16)
            assert lst.idx == 0, f"lane {lane} semaphore overflow"
            inc = (rec[1], 16)
        elif wts:
            rec = self.streams[eng].bump(1)
            inc = (rec[1], 1)
        else:
            rec = None
            inc = None

        def thunk(e, _waits=waits, _build=build, _inc=inc):
            for sem, val in _waits:
                e.wait_ge(sem, val)
            ins = _build(e)
            if _inc is not None and ins is not None:
                ins.then_inc(_inc[0], _inc[1])

        self.thunks[eng].append(thunk)
        for t in wts:
            t.writer = rec
            t.readers = []
        for t in rts:
            if rec is not None:
                t.readers.append(rec)
        return rec

    @contextlib.contextmanager
    def phase(self, name):
        yield
        self.flush_phase(name)

    def flush_phase(self, name):
        nc = self.nc
        todo = {e: th for e, th in self.thunks.items() if th}
        self.thunks = {e: [] for e in ENG2BODY}
        if todo:
            with nc.Block(f"ph{self.phase_idx}_{name}") as blk:
                for eng, ths in todo.items():
                    def body(engine, _ths=ths):
                        for th in _ths:
                            th(engine)
                    getattr(blk, ENG2BODY[eng])(body)
        self.phase_idx += 1


    def tt(self, eng, out, in0, in1, op):
        _, oa = _tv(out); _, a0 = _tv(in0); _, a1 = _tv(in1)
        self.emit(eng, lambda e: e.tensor_tensor(oa, a0, a1, op),
                  reads=[in0, in1], writes=[out])

    def ts(self, eng, out, in0, s1, s2=None, op0=OP.mult, op1=OP.mult):
        _, oa = _tv(out); _, a0 = _tv(in0)
        r = [in0]
        s1t, s1a = _tv(s1) if isinstance(s1, (Tile, tuple)) else (None, s1)
        if s1t is not None:
            r.append(s1)
        s2a = s2
        if isinstance(s2, (Tile, tuple)):
            s2t, s2a = _tv(s2)
            r.append(s2)
        if s2 is None:
            self.emit(eng, lambda e: e.tensor_single_scalar(oa, a0, s1a, op0),
                      reads=r, writes=[out])
        else:
            self.emit(eng, lambda e: e.tensor_scalar(oa, a0, s1a, s2a, op0, op1),
                      reads=r, writes=[out])

    def stt(self, eng, out, in0, scalar, in1, op0, op1):
        _, oa = _tv(out); _, a0 = _tv(in0); _, a1 = _tv(in1)
        r = [in0, in1]
        st, sa = _tv(scalar) if isinstance(scalar, (Tile, tuple)) else (None, scalar)
        if st is not None:
            r.append(scalar)
        self.emit(eng, lambda e: e.scalar_tensor_tensor(oa, a0, sa, a1, op0, op1),
                  reads=r, writes=[out])

    def ccol(self, val):
        if not hasattr(self, "_ccols"):
            self._ccols = {}
            self._ccol_region = self.alloc("ccols", F32, 128, 32)
            self._ccol_n = 0
        if val not in self._ccols:
            i = self._ccol_n
            self._ccol_n += 1
            t = Tile(f"cc_{val}", self._ccol_region.ap[:, i:i + 1])
            self.memset("dve", t, val)
            self._ccols[val] = t
        return self._ccols[val]

    def _constify(self, v, in_ap, reads):
        if isinstance(v, (int, float)) and float(v) not in (0.0, 1.0):
            t = self.ccol(float(v))
            b = in_ap.base_partition()
            p = in_ap.partition_size()
            reads.append(t)
            return t.ap[b:b + p, 0:1] if (b, p) != (0, 128) else t.ap
        return v

    def act(self, eng, out, in0, func, bias=0.0, scale=1.0, accum_out=None):
        _, oa = _tv(out); _, a0 = _tv(in0)
        r = [in0]
        ba = bias
        if isinstance(bias, (Tile, tuple)):
            bt, ba = _tv(bias); r.append(bias)
        else:
            ba = self._constify(bias, a0, r)
        sa = scale
        if isinstance(scale, (Tile, tuple)):
            stl, sa = _tv(scale); r.append(scale)
        else:
            sa = self._constify(scale, a0, r)
        w = [out]
        ao = None
        if accum_out is not None:
            aot, ao = _tv(accum_out)
            w.append(accum_out)
        self.emit(eng, lambda e: e.activation(oa, a0, func, bias=ba, scale=sa, accum_out=ao),
                  reads=r, writes=w)

    def copy(self, eng, out, in0):
        _, oa = _tv(out); _, a0 = _tv(in0)
        self.emit(eng, lambda e: e.tensor_copy(oa, a0), reads=[in0], writes=[out])

    def memset(self, eng, out, val):
        _, oa = _tv(out)
        self.emit(eng, lambda e: e.memset(oa, val), reads=[], writes=[out])

    def scan(self, eng, out, d0, d1, initial, op0=OP.mult, op1=OP.add):
        _, oa = _tv(out); _, a0 = _tv(d0); _, a1 = _tv(d1)
        r = [d0, d1]
        ia = initial
        if isinstance(initial, (Tile, tuple)):
            it_, ia = _tv(initial); r.append(initial)
        self.emit(eng, lambda e: e.tensor_tensor_scan(oa, a0, a1, ia, op0, op1),
                  reads=r, writes=[out])

    def red(self, eng, out, in0, op, absval=None, negate=None):
        _, oa = _tv(out); _, a0 = _tv(in0)
        self.emit(eng, lambda e: e.tensor_reduce(oa, a0, AX.X, op,
                                                 apply_absolute_value=absval, negate=negate),
                  reads=[in0], writes=[out])

    def recip(self, out, in0):
        _, oa = _tv(out); _, a0 = _tv(in0)
        self.emit("dve", lambda e: e.reciprocal(oa, a0), reads=[in0], writes=[out])

    def mm(self, out, lhsT, rhs, start, stop, inc):
        ot, oa = _tv(out); _, la = _tv(lhsT); _, ra = _tv(rhs)
        if inc:
            return self.emit("pe", lambda e: e.matmul(oa, la, ra, start=start, stop=stop),
                             reads=[lhsT, rhs], writes=[out])
        else:
            def b(e, _oa=oa, _la=la, _ra=ra, _s=start, _st=stop):
                e.matmul(_oa, _la, _ra, start=_s, stop=_st)
                return None
            self.emit("pe", b, reads=[lhsT, rhs], writes=[], war=[out] if start else [])
            return None

    def _attach_group_reads(self, out_tile, tiles):
        t, _ = _tv(out_tile)
        rec = t.writer
        assert rec is not None
        for x in tiles:
            tt_, _ = _tv(x)
            if tt_ is not None and (not tt_.readers or tt_.readers[-1] != rec):
                tt_.readers.append(rec)

    def mm_group(self, out, pairs, extra=None):
        allp = list(pairs) + (list(extra) if extra else [])
        n = len(allp)
        for i, (l, r) in enumerate(allp):
            self.mm(out, l, r, start=(i == 0), stop=(i == n - 1), inc=(i == n - 1))
        for l, r in allp:
            self._attach_group_reads(out, [l, r])

    def transpose(self, out, in0, ident):
        _, oa = _tv(out); _, a0 = _tv(in0); _, ia = _tv(ident)
        self.emit("pe", lambda e: e.transpose(oa, a0, ia),
                  reads=[in0, ident], writes=[out])

    def dma(self, out, in0, queue="sp"):
        _, oa = _tv(out); _, a0 = _tv(in0)
        return self.emit(queue, lambda e: e.dma_start(oa, a0),
                         reads=[in0], writes=[out], dmaq=DMAQ[queue])

    def iota(self, out, pattern, cm=0):
        _, oa = _tv(out)
        self.emit("pool", lambda e: e.iota(oa, pattern, channel_multiplier=cm,
                                           allow_small_or_imprecise_dtypes=True),
                  reads=[], writes=[out])

    def bank(self):
        t = self.psum[self._psum_rr % 8]
        self._psum_rr += 1
        return t

    def banks(self, n):
        return [self.bank() for _ in range(n)]

    def alloc(self, name, dtype, rows, cols):
        return self.arena.alloc(self, name, dtype, rows, cols)

    def drain_all_dma(self):
        for q, n in NLANES.items():
            for i in range(n):
                s = self.streams[f"{q}{i}"]
                if s.val > 0:
                    self.emit("sp", lambda e, _s=s.sem, _v=s.val: e.wait_ge(_s, _v) and None,
                              reads=[], writes=[])



def _t(a):
    return np.ascontiguousarray(np.asarray(a).T)

def _pcol(a, nt):
    return np.ascontiguousarray(np.asarray(a).reshape(nt, 128).T)

def _pcol2(a, nt, k):
    return np.ascontiguousarray(np.asarray(a).reshape(nt, 128, k).transpose(1, 0, 2).reshape(128, nt * k))


def prepare_weights(inp):
    w = {}
    w["l0_win"] = np.asarray(inp["L0_attn_W_in"])
    w["l0_convw"] = _pcol2(np.asarray(inp["L0_attn_conv_w"]).reshape(DI, 4), DIT, 4)
    w["l0_convb"] = _pcol(inp["L0_attn_conv_b"], DIT)
    w["l0_wx"] = np.asarray(inp["L0_attn_W_x"])
    w["l0_alog"] = _pcol2(inp["L0_attn_A_log"], DIT, NSTATE)
    w["l0_d"] = _pcol(inp["L0_attn_D_param"], DIT)
    w["l0_dtb_row"] = np.asarray(inp["L0_attn_dt_bias"]).reshape(1, DI)
    w["l0_wout"] = np.asarray(inp["L0_attn_W_out"])
    w["l0f_w1"] = _t(inp["L0_ffn_w1_w"])
    w["l0f_w2"] = _t(inp["L0_ffn_w2_w"])
    w["l0f_h1"] = _t(inp["L0_ffn_h1_w"])
    w["l0f_h2"] = _t(inp["L0_ffn_h2_w"])
    w["l0f_w3"] = _t(inp["L0_ffn_w3_w"])
    w["l0f_s1"] = _pcol(inp["L0_ffn_w1_s"], 32)
    w["l0f_s2"] = _pcol(inp["L0_ffn_w2_s"], 32)
    w["l0f_hs1"] = _pcol(inp["L0_ffn_h1_s"], 32)
    w["l0f_hs2"] = _pcol(inp["L0_ffn_h2_s"], 32)
    w["l0f_s3"] = _pcol(inp["L0_ffn_w3_s"], DT)
    w["l1_g"] = _pcol(inp["L1_attn_norm_g"], DT)
    w["l1_b"] = _pcol(inp["L1_attn_norm_b"], DT)
    for pfx in ("f", "b"):
        w[f"l1{pfx}_win"] = _t(inp[f"L1_attn_{pfx}_Win"])
        w[f"l1{pfx}_convw"] = _pcol2(np.asarray(inp[f"L1_attn_{pfx}_convw"]).reshape(DI, 4), DIT, 4)
        w[f"l1{pfx}_convb"] = _pcol(inp[f"L1_attn_{pfx}_convb"], DIT)
        w[f"l1{pfx}_wx"] = _t(inp[f"L1_attn_{pfx}_Wx"])
        w[f"l1{pfx}_wdt"] = _t(inp[f"L1_attn_{pfx}_Wdt"])
        w[f"l1{pfx}_bdt"] = np.asarray(inp[f"L1_attn_{pfx}_bdt"]).reshape(1, DI)
        w[f"l1{pfx}_alog"] = _pcol2(inp[f"L1_attn_{pfx}_Alog"], DIT, NSTATE)
        w[f"l1{pfx}_d"] = _pcol(inp[f"L1_attn_{pfx}_D"], DIT)
        w[f"l1{pfx}_wout"] = _t(inp[f"L1_attn_{pfx}_Wout"])
    w["l1f_wg"] = np.asarray(inp["L1_ffn_Wg"])
    w["l1f_bg"] = np.asarray(inp["L1_ffn_bg"]).reshape(1, 8)
    w["l1f_w1"] = np.asarray(inp["L1_ffn_W1"])
    w["l1f_b1"] = np.stack([_pcol(np.asarray(inp["L1_ffn_b1"])[e], 32) for e in range(8)])
    w["l1f_w2"] = np.asarray(inp["L1_ffn_W2"])
    w["l1f_b2"] = np.asarray(inp["L1_ffn_b2"])
    w["l1f_gam"] = _pcol(inp["L1_ffn_gamma"], DT)
    w["l1f_bet"] = _pcol(inp["L1_ffn_beta"], DT)
    w["l2_w0"] = np.asarray(inp["L2_attn_W0"])
    w["l2_ainit"] = np.asarray(inp["L2_attn_A_init"])
    w["l2_binit"] = np.asarray(inp["L2_attn_B_init"])
    w["l2_loglr"] = _pcol(inp["L2_attn_log_lr"], DT)
    for nm in ("Wq", "Wk", "Wv", "Wo", "Wg"):
        w[f"l2_{nm.lower()}"] = _t(inp[f"L2_attn_{nm}"])
    w["l2_lng"] = _pcol(inp["L2_attn_ln_g"], DT)
    w["l2_lnb"] = _pcol(inp["L2_attn_ln_b"], DT)
    w["l2_lsg"] = _pcol(inp["L2_attn_ls_g"], DT)
    w["l2f_w"] = _t(inp["L2_ffn_W"])
    w["l2f_b"] = np.asarray(inp["L2_ffn_b"]).reshape(1, D)
    w["l2f_ldown"] = np.asarray(inp["L2_ffn_lora_down"])
    w["l2f_lupT"] = _t(inp["L2_ffn_lora_up"])
    w["l2f_daux"] = np.asarray(inp["L2_ffn_down_aux"])
    w["l2f_uauxT"] = _t(inp["L2_ffn_up_aux"])
    w["l3_wqkv"] = np.asarray(inp["L3_attn_w_qkv"])
    w["l3_proj"] = np.asarray(inp["L3_attn_proj"])
    w["l3_wout"] = np.asarray(inp["L3_attn_w_out"])
    w["l3_bout"] = np.asarray(inp["L3_attn_b_out"]).reshape(1, D)
    w["l3f_w1"] = _t(inp["L3_ffn_W1"])
    w["l3f_b1"] = np.asarray(inp["L3_ffn_b1"]).reshape(1, 4096)
    w["l3f_w2"] = _t(inp["L3_ffn_W2"])
    w["l3f_b2"] = np.asarray(inp["L3_ffn_b2"]).reshape(1, D)
    for k in w:
        w[k] = np.ascontiguousarray(w[k], dtype=np.float32)
    return w


WSPECS = None

def _wspecs(w):
    return {k: list(v.shape) for k, v in w.items()}



class Prog:

    def __init__(self, nc, ctx, win, dbg):
        self.nc = nc
        self.ctx = ctx
        self.win = win
        self.dbg = dbg
        self.dram = {}

    def dram_scratch(self, name, rows, cols, dtype=F32):
        kind = "ExternalOutput" if name in self.dbg else "Internal"
        h = self.nc.dram_tensor(name, [rows, cols], dtype, kind=kind)
        tiles = {}
        self.dram[name] = (h, tiles)
        return h

    def dtile(self, name, key, ap):
        h, tiles = self.dram[name]
        if key not in tiles:
            assert ap is not None, f"dtile {name}:{key} looked up before creation"
            tiles[key] = Tile(f"{name}:{key}", ap)
        return tiles[key]



def emit_setup(P):
    ctx = P.ctx
    g = P.g = {}
    g["ones_row_f"] = ctx.alloc("ones_row_f", F32, 128, CH)
    g["ones_row_b"] = ctx.alloc("ones_row_b", BF16, 128, CH)
    g["ones128_f"] = ctx.alloc("ones128_f", F32, 128, 128)
    g["ones_col_f"] = ctx.alloc("ones_col_f", F32, 128, 1)
    g["ident_f"] = ctx.alloc("ident_f", F32, 128, 128)
    g["ident_b"] = ctx.alloc("ident_b", BF16, 128, 128)
    g["sel16"] = ctx.alloc("sel16", F32, 128, 16 * 128)
    g["selW"] = ctx.alloc("selW", F32, 128, 7 * 8)
    scr = ctx.alloc("setup_scr", F32, 128, 2048)
    with ctx.phase("setup"):
        ctx.memset("dve", g["ones_row_f"], 1.0)
        ctx.memset("dve", g["ones_row_b"], 1.0)
        ctx.memset("dve", g["ones128_f"], 1.0)
        ctx.memset("dve", g["ones_col_f"], 1.0)
        ctx.iota((ctx.iotmp, ctx.iotmp.ap[:, 0:128]), [[1, 128]], cm=-1)
        ctx.ts("dve", (g["ident_f"], g["ident_f"].ap), (ctx.iotmp, ctx.iotmp.ap[:, 0:128]), 0.0, None, OP.is_equal)
        ctx.copy("dve", g["ident_b"], g["ident_f"])
        for i in range(16):
            ctx.ts("dve", (g["sel16"], g["sel16"].ap[:, i * 128:(i + 1) * 128]),
                   (g["ones128_f"], g["ones128_f"].ap), (g["ident_f"], g["ident_f"].ap[:, i:i + 1]),
                   None, OP.mult)
        ctx.memset("dve", g["selW"], 0.0)
        for w in range(7):
            ctx.memset("dve", (g["selW"], g["selW"].ap[:, w * 8 + w: w * 8 + w + 1]), 1.0)
    scr.free()


def bcast_row(P, psum, row_view, n=CH):
    ctx = P.ctx
    _, rap = _tv(row_view)
    b = rap.base_partition()
    ones = P.g["ones128_f"]
    ctx.mm((psum, psum.ap[:, 0:n]), (ones, ones.ap[b:b + 1, :]),
           row_view, start=True, stop=True, inc=True)


def ones_rhs(P, lhsT_view, n=CH):
    _, lap = _tv(lhsT_view)
    b = lap.base_partition()
    t = P.g["ones_row_f"]
    return (t, t.ap[b:b + 1, 0:n])



def emit_prelude(P, xin, need_rho):
    ctx = P.ctx
    xin_r = xin.ap().rearrange("(dt p) t -> p dt t", p=128)
    for c in range(NCH):
        with ctx.phase(f"pre{c}"):
            xf = ctx.alloc("pre_xf", F32, 128, DT * CH)
            sq = ctx.alloc("pre_sq", F32, 128, DT * CH)
            xt = P.dtile(P.xin_name, ("c", c), xin_r[:, :, c * CH:(c + 1) * CH])
            ctx.dma((xf, xf.ap.rearrange("p (dt t) -> p dt t", dt=DT)), xt, queue="sp")
            psA, psB = ctx.bank(), ctx.bank()
            ctx.mm_group((psA, psA.ap[0:1, :]),
                         [(P.g["ones_col_f"], (xf, xf.ap[:, dt * CH:(dt + 1) * CH])) for dt in range(DT)])
            for dt in range(DT):
                ctx.act("act", (sq, sq.ap[:, dt * CH:(dt + 1) * CH]),
                        (xf, xf.ap[:, dt * CH:(dt + 1) * CH]), AF.Square)
            ctx.mm_group((psB, psB.ap[0:1, :]),
                         [(P.g["ones_col_f"], (sq, sq.ap[:, dt * CH:(dt + 1) * CH])) for dt in range(DT)])
            mv = (P.m_row, P.m_row.ap[0:1, c * CH:(c + 1) * CH])
            rv = (P.r_row, P.r_row.ap[0:1, c * CH:(c + 1) * CH])
            t1 = ctx.alloc("pre_t1", F32, 1, CH)
            t2 = ctx.alloc("pre_t2", F32, 1, CH)
            ctx.act("act", mv, (psA, psA.ap[0:1, :]), AF.Copy, scale=1.0 / D)
            ctx.act("act", (t1, t1.ap), (psB, psB.ap[0:1, :]), AF.Copy, scale=1.0 / D)
            ctx.act("act", (t2, t2.ap), mv, AF.Square)
            ctx.tt("dve", (t1, t1.ap), (t1, t1.ap), (t2, t2.ap), OP.subtract)
            ctx.act("act", (t2, t2.ap), (t1, t1.ap), AF.Ln, bias=EPS)
            ctx.act("act", rv, (t2, t2.ap), AF.Exp, scale=-0.5)
            if need_rho:
                rhov = (P.rho_row, P.rho_row.ap[0:1, c * CH:(c + 1) * CH])
                t3 = ctx.alloc("pre_t3", F32, 1, CH)
                ctx.ts("dve", (t3, t3.ap), (t1, t1.ap), EPS, None, OP.add)
                ctx.recip((t3, t3.ap), (t3, t3.ap))
                ctx.tt("dve", (t3, t3.ap), (t1, t1.ap), (t3, t3.ap), OP.mult)
                ctx.act("act", (t3, t3.ap), (t3, t3.ap), AF.Ln, bias=EPS)
                ctx.act("act", rhov, (t3, t3.ap), AF.Exp, scale=-0.5)
                t3.free()
            psM, psR = ctx.bank(), ctx.bank()
            bcast_row(P, psM, mv)
            bcast_row(P, psR, rv)
            xs = ctx.alloc("pre_xn", BF16, 128, DT * CH)
            for dt in range(DT):
                xv = (xf, xf.ap[:, dt * CH:(dt + 1) * CH])
                sv = (sq, sq.ap[:, dt * CH:(dt + 1) * CH])
                ctx.tt("dve", sv, xv, (psM, psM.ap[:, :]), OP.subtract)
                ctx.tt("dve", (xs, xs.ap[:, dt * CH:(dt + 1) * CH]),
                       sv, (psR, psR.ap[:, :]), OP.mult)
            xnt = P.dtile("xn", ("c", c), P.xn_h.ap().rearrange(
                "(dt p) t -> p dt t", p=128)[:, :, c * CH:(c + 1) * CH])
            ctx.dma(xnt, (xs, xs.ap.rearrange("p (dt t) -> p dt t", dt=DT)), queue="sp")
            t1.free(); t2.free(); xf.free(); sq.free(); xs.free()



def stage_bf16(P, src_ap, name, R, C, scale_col=None):
    ctx = P.ctx
    h = P.dram_scratch(name, R, C, BF16)
    nk = R // 128
    csz = min(C, 8192)
    stg = [ctx.alloc(f"stg{i}", BF16, 128, csz) for i in range(2)]
    stf = [ctx.alloc(f"stf{i}", F32, 128, csz) for i in range(2)] if scale_col is not None else None
    with ctx.phase(f"stage_{name}"):
        i = 0
        for kt in range(nk):
            for cc in range(0, C, csz):
                w = min(csz, C - cc)
                s = stg[i % 2]
                if scale_col is None:
                    ctx.dma((s, s.ap[:, 0:w]), src_ap[kt * 128:(kt + 1) * 128, cc:cc + w], queue="pool")
                else:
                    f = stf[i % 2]
                    ctx.dma((f, f.ap[:, 0:w]), src_ap[kt * 128:(kt + 1) * 128, cc:cc + w], queue="sp")
                    sc_t, sc_ap = scale_col
                    ctx.act("act", (s, s.ap[:, 0:w]), (f, f.ap[:, 0:w]), AF.Copy,
                            scale=(sc_t, sc_ap(kt)))
                dst = P.dtile(name, (kt, cc), h[kt * 128:(kt + 1) * 128, cc:cc + w])
                rec = ctx.dma(dst, (s, s.ap[:, 0:w]), queue="sp")
                i += 1
    for s in stg:
        s.free()
    if stf:
        for f in stf:
            f.free()
    summary = P.dtile(name, "ALL", h[:, :])
    summary.writer = rec
    summary.handle = h
    return summary


class RowPool:

    def __init__(self, ctx, region_cols=(4096, 4096)):
        self.ctx = ctx
        self.regions = [ctx.alloc(f"rowpool{i}", F32, 128, c) for i, c in enumerate(region_cols)]
        self.freemap = []
        for r, c in zip(self.regions, region_cols):
            for slot in (0, 32, 64):
                self.freemap.append([r, slot, 0, c])

    def alloc(self, name, cols):
        for ent in self.freemap:
            r, slot, used, cap = ent
            if cap - used >= cols:
                ap = r.ap[slot:slot + 1, used:used + cols]
                ent[2] = used + cols
                t = Tile(f"{name}", ap)
                t._rp = (ent, used, cols)
                t.readers = list(getattr(self, "_hazards", {}).get((id(ent[0]), slot), []))
                return t
        raise MemoryError(f"rowpool: no room for {name} ({cols})")

    def free(self, tile):
        ent, start, cols = tile._rp
        if not hasattr(self, "_hazards"):
            self._hazards = {}
        hz = self._hazards.setdefault((id(ent[0]), ent[1]), [])
        hz.extend(tile.readers)
        if tile.writer:
            hz.append(tile.writer)
        del hz[:-12]
        if ent[2] == start + cols:
            ent[2] = start

    def reset(self):
        for ent in self.freemap:
            ent[2] = 0



def emit_mamba(P, tag, cfg):
    ctx = P.ctx
    g = P.g
    no = cfg["no"]
    rev = cfg["rev"]
    zs_h = P.dram_scratch(cfg["zs_name"], DI, T)
    attn_h = P.dram[cfg["attn_name"]][0]
    win_t = cfg["win_bf"]
    wout_t = cfg["wout_bf"]
    win_r = win_t.handle.ap().rearrange("(kt p) e -> p kt e", p=128)
    wout_r = wout_t.handle.ap().rearrange("(kt p) m -> p kt m", p=128)
    if f"{tag}_dbg_y" in P.dbg:
        P.dram_scratch(f"{tag}_dbg_y", DI, T)
        P.dram_scratch(f"{tag}_dbg_dt", DI, T)
    state = ctx.alloc(f"{tag}_state", F32, 128, DIT * NSTATE)
    xh = ctx.alloc(f"{tag}_xh", F32, 128, DIT * 3)
    attn_acc = ctx.alloc(f"{tag}_aacc", F32, 128, DT * CH)
    with ctx.phase(f"{tag}_init"):
        ctx.memset("dve", state, 0.0)
        ctx.memset("dve", xh, 0.0)

    for c in range(NCH):
        cc = (NCH - 1 - c) if rev else c
        with ctx.phase(f"{tag}_p1_{c}"):
            xc = ctx.alloc(f"{tag}_xc", F32, 128, DIT * CH)
            xnc = ctx.alloc(f"{tag}_xnc", BF16, 128, DT * CH)
            xnt = P.dtile("xn", ("c", cc), P.xn_h.ap().rearrange(
                "(dt p) t -> p dt t", p=128)[:, :, cc * CH:(cc + 1) * CH])
            ctx.dma((xnc, xnc.ap.rearrange("p (dt t) -> p dt t", dt=DT)), xnt, queue="sp")
            if cfg["rho_mul"]:
                xnr = ctx.alloc(f"{tag}_xnr", BF16, 128, DT * CH)
                psR = ctx.bank()
                rv = (P.rho_row, P.rho_row.ap[:, cc * CH:(cc + 1) * CH])
                bcast_row(P, psR, rv)
                psv = psR.ap[:, ::-1] if rev else psR.ap[:, :]
                for dt in range(DT):
                    src = xnc.ap[:, dt * CH:(dt + 1) * CH]
                    if rev:
                        src = src[:, ::-1]
                    ctx.tt("dve", (xnr, xnr.ap[:, dt * CH:(dt + 1) * CH]),
                           (xnc, src), (psR, psv), OP.mult)
                rhs_t = lambda kt: (xnr, xnr.ap[:, kt * CH:(kt + 1) * CH])
            elif rev:
                xnr = ctx.alloc(f"{tag}_xnr", BF16, 128, DT * CH)
                for dt in range(DT):
                    src = xnc.ap[:, dt * CH:(dt + 1) * CH][:, ::-1]
                    ctx.copy("dve", (xnr, xnr.ap[:, dt * CH:(dt + 1) * CH]), (xnc, src))
                rhs_t = lambda kt: (xnr, xnr.ap[:, kt * CH:(kt + 1) * CH])
            else:
                xnr = None
                rhs_t = lambda kt: (xnc, xnc.ap[:, kt * CH:(kt + 1) * CH])
            xb = ctx.alloc(f"{tag}_xb", F32, 128, DIT * 515)
            for dt in range(DIT):
                ctx.copy("dve", (xb, xb.ap[:, dt * 515:dt * 515 + 3]),
                         (xh, xh.ap[:, dt * 3:dt * 3 + 3]))
            pan = [ctx.alloc(f"{tag}_pan{i}", BF16, 128, DT * 128) for i in range(2)]
            zst = [ctx.alloc(f"{tag}_zst{i}", F32, 128, CH) for i in range(2)]
            for m in range(2 * DIT):
                p = pan[m % 2]
                ctx.dma((p, p.ap.rearrange("p (kt e) -> p kt e", kt=DT)),
                        (win_t, win_r[:, :, m * 128:(m + 1) * 128]), queue="sp")
                ps = ctx.bank()
                pairs = [((p, p.ap[:, kt * 128:(kt + 1) * 128]), rhs_t(kt)) for kt in range(DT)]
                cr = cfg["const_row"]
                crv = cr and (cr, cr.ap[0:1, m * 128:(m + 1) * 128])
                ctx.mm_group(ps, pairs, extra=cr and [(crv, ones_rhs(P, crv))])
                if m < DIT:
                    ctx.act("act", (xb, xb.ap[:, m * 515 + 3:m * 515 + 515]), ps, AF.Copy)
                else:
                    z = zst[m % 2]
                    ctx.act("act", z, ps, AF.Silu)
                    zt = P.dtile(cfg["zs_name"], (m - DIT, c),
                                 zs_h[(m - DIT) * 128:(m - DIT + 1) * 128, c * CH:(c + 1) * CH])
                    ctx.dma(zt, z, queue="sp")
            for dt in range(DIT):
                ctx.copy("dve", (xh, xh.ap[:, dt * 3:dt * 3 + 3]),
                         (xb, xb.ap[:, dt * 515 + 512:dt * 515 + 515]))
                acc_v = (xc, xc.ap[:, dt * CH:(dt + 1) * CH])
                ctx.ts("dve", acc_v, (xb, xb.ap[:, dt * 515 + 0:dt * 515 + 512]),
                       (cfg["convw"], cfg["convw"].ap[:, dt * 4:dt * 4 + 1]), None, OP.mult)
                for k in range(1, 4):
                    ctx.stt("dve", acc_v, (xb, xb.ap[:, dt * 515 + k:dt * 515 + k + 512]),
                            (cfg["convw"], cfg["convw"].ap[:, dt * 4 + k:dt * 4 + k + 1]),
                            acc_v, OP.mult, OP.add)
                ctx.act("act", acc_v, acc_v, AF.Silu,
                        bias=(cfg["convb"], cfg["convb"].ap[:, dt:dt + 1]))
            xb.free()
            for p_ in pan:
                p_.free()
            for z_ in zst:
                z_.free()
            xnc.free()
            if xnr is not None:
                xnr.free()

        with ctx.phase(f"{tag}_p2_{c}"):
            xcb = ctx.alloc(f"{tag}_xcb", BF16, 128, DIT * CH)
            for dt in range(DIT):
                ctx.copy("dve", (xcb, xcb.ap[:, dt * CH:(dt + 1) * CH]),
                         (xc, xc.ap[:, dt * CH:(dt + 1) * CH]))

            def wxg(lo, hi):
                ps_ = ctx.bank()
                n_ = hi - lo
                ctx.mm_group((ps_, ps_.ap[0:n_, :]),
                             [((cfg["wx"], cfg["wx"].ap[:, kt * no + lo: kt * no + hi]),
                               (xcb, xcb.ap[:, kt * CH:(kt + 1) * CH])) for kt in range(DIT)])
                return ps_
            ssmB = ctx.alloc(f"{tag}_ssmB", F32, 16, CH)
            ssmC = ctx.alloc(f"{tag}_ssmC", F32, 16, CH)
            if cfg["variant"] == "v1":
                psB0 = wxg(0, 16)
                psC0 = wxg(16, 32)
                psT0 = wxg(32, 33)
                dtw = ctx.alloc(f"{tag}_dtw", F32, 1, CH)
                dtr = None
                ctx.act("act", dtw, (psT0, psT0.ap[0:1, :]), AF.Copy)
            else:
                psR0 = wxg(0, 64)
                psB0 = wxg(64, 80)
                psC0 = wxg(80, 96)
                dtw = None
                dtr = ctx.alloc(f"{tag}_dtr", BF16, 64, CH)
                ctx.act("act", (dtr, dtr.ap[0:64, :]), (psR0, psR0.ap[0:64, :]), AF.Copy)
            ctx.act("act", (ssmB, ssmB.ap[0:16, :]), (psB0, psB0.ap[0:16, :]), AF.Copy)
            ctx.act("act", (ssmC, ssmC.ap[0:16, :]), (psC0, psC0.ap[0:16, :]), AF.Copy)
            xcb.free()

        with ctx.phase(f"{tag}_p3_{c}"):
            yfb = ctx.alloc(f"{tag}_yfb", BF16, 128, 8 * CH)
            dt_g = ctx.alloc(f"{tag}_dt", F32, 128, 8 * CH)
            u_g = ctx.alloc(f"{tag}_u", F32, 128, 8 * CH)
            sp_s = ctx.alloc(f"{tag}_sps", F32, 128, CH)
            sp_t = ctx.alloc(f"{tag}_spt", F32, 128, CH)
            a_scr = ctx.alloc(f"{tag}_a", F32, 128, CH)
            b_scr = ctx.alloc(f"{tag}_b", F32, 128, CH)
            h_scr = ctx.alloc(f"{tag}_h", F32, 128, CH)
            bbc = ctx.alloc(f"{tag}_bbc", F32, 128, CH)
            cbc = ctx.alloc(f"{tag}_cbc", F32, 128, CH)
            y_g = ctx.alloc(f"{tag}_y", F32, 128, 8 * CH)
            zres = [ctx.alloc(f"{tag}_zr{i}", F32, 128, CH) for i in range(2)]
            for gi in range(2):
                for dloc in range(8):
                    di = gi * 8 + dloc
                    psD = ctx.bank()
                    dv = (dt_g, dt_g.ap[:, dloc * CH:(dloc + 1) * CH])
                    if cfg["variant"] == "v1":
                        ctx.mm(psD, (g["ones128_f"], g["ones128_f"].ap[0:1, :]),
                               (dtw, dtw.ap[0:1, :]), start=True, stop=False, inc=False)
                        dtbv = (cfg["dtb_row"], cfg["dtb_row"].ap[0:1, di * 128:(di + 1) * 128])
                        ctx.mm(psD, dtbv, ones_rhs(P, dtbv), start=False, stop=True, inc=True)
                        ctx._attach_group_reads(psD, [g["ones128_f"], dtw])
                    else:
                        ctx.mm(psD, (cfg["wdt"], cfg["wdt"].ap[0:64, di * 128:(di + 1) * 128]),
                               (dtr, dtr.ap[0:64, :]), start=True, stop=False, inc=False)
                        bdtv = (cfg["bdt_row"], cfg["bdt_row"].ap[0:1, di * 128:(di + 1) * 128])
                        ctx.mm(psD, bdtv, ones_rhs(P, bdtv), start=False, stop=True, inc=True)
                        ctx._attach_group_reads(psD, [cfg["wdt"], dtr])
                    ctx.act("act", sp_s, psD, AF.Copy)
                    ctx.act("act", sp_t, sp_s, AF.Abs)
                    ctx.act("act", sp_t, sp_t, AF.Exp, scale=-1.0)
                    ctx.act("act", sp_t, sp_t, AF.Ln, bias=1.0)
                    ctx.ts("dve", dv, sp_s, 0.0, None, OP.max)
                    ctx.tt("dve", dv, dv, sp_t, OP.add)
                    xv = (xc, xc.ap[:, di * CH:(di + 1) * CH])
                    ctx.tt("dve", (u_g, u_g.ap[:, dloc * CH:(dloc + 1) * CH]), dv, xv, OP.mult)
                    ctx.ts("dve", xv, xv, (cfg["d_par"], cfg["d_par"].ap[:, di:di + 1]), None, OP.mult)
                for s in range(NSTATE):
                    psB_, psC_ = ctx.bank(), ctx.bank()
                    ctx.mm(psB_, (g["sel16"], g["sel16"].ap[0:16, s * 128:(s + 1) * 128]),
                           (ssmB, ssmB.ap[0:16, :]), start=True, stop=True, inc=True)
                    ctx.mm(psC_, (g["sel16"], g["sel16"].ap[0:16, s * 128:(s + 1) * 128]),
                           (ssmC, ssmC.ap[0:16, :]), start=True, stop=True, inc=True)
                    ctx.act("act", bbc, psB_, AF.Copy)
                    ctx.act("act", cbc, psC_, AF.Copy)
                    for dloc in range(8):
                        di = gi * 8 + dloc
                        dv = (dt_g, dt_g.ap[:, dloc * CH:(dloc + 1) * CH])
                        ctx.act("act", a_scr, dv, AF.Exp,
                                scale=(cfg["a_neg"], cfg["a_neg"].ap[:, di * NSTATE + s: di * NSTATE + s + 1]))
                        ctx.tt("dve", b_scr, (u_g, u_g.ap[:, dloc * CH:(dloc + 1) * CH]), bbc, OP.mult)
                        sv = (state, state.ap[:, di * NSTATE + s: di * NSTATE + s + 1])
                        ctx.scan("dve", h_scr, a_scr, b_scr, sv)
                        ctx.copy("dve", sv, (h_scr, h_scr.ap[:, CH - 1:CH]))
                        yv = (y_g, y_g.ap[:, dloc * CH:(dloc + 1) * CH])
                        if s == 0:
                            ctx.tt("dve", yv, h_scr, cbc, OP.mult)
                        else:
                            ctx.tt("dve", b_scr, h_scr, cbc, OP.mult)
                            ctx.tt("dve", yv, yv, b_scr, OP.add)
                if f"{tag}_dbg_y" in P.dbg:
                    ydb_h = P.dram[f"{tag}_dbg_y"][0]
                    dtdb_h = P.dram[f"{tag}_dbg_dt"][0]
                    for dloc in range(8):
                        di = gi * 8 + dloc
                        ydt = P.dtile(f"{tag}_dbg_y", (di, c), ydb_h[di * 128:(di + 1) * 128, c * CH:(c + 1) * CH])
                        ctx.dma(ydt, (y_g, y_g.ap[:, dloc * CH:(dloc + 1) * CH]), queue="sp")
                        ddt = P.dtile(f"{tag}_dbg_dt", (di, c), dtdb_h[di * 128:(di + 1) * 128, c * CH:(c + 1) * CH])
                        ctx.dma(ddt, (dt_g, dt_g.ap[:, dloc * CH:(dloc + 1) * CH]), queue="sp")
                for dloc in range(8):
                    di = gi * 8 + dloc
                    yv = (y_g, y_g.ap[:, dloc * CH:(dloc + 1) * CH])
                    xv = (xc, xc.ap[:, di * CH:(di + 1) * CH])
                    ctx.tt("dve", yv, yv, xv, OP.add)
                    z = zres[dloc % 2]
                    zt = P.dtile(cfg["zs_name"], (di, c), zs_h[di * 128:(di + 1) * 128, c * CH:(c + 1) * CH])
                    ctx.dma(z, zt, queue="sp")
                    ctx.tt("dve", (yfb, yfb.ap[:, dloc * CH:(dloc + 1) * CH]), yv, z, OP.mult)
                wp = ctx.alloc(f"{tag}_wp", BF16, 128, 8 * 128)
                for mo in range(DT):
                    ctx.dma((wp, wp.ap.rearrange("p (kt m) -> p kt m", kt=8)),
                            (wout_t, wout_r[:, gi * 8:(gi + 1) * 8, mo * 128:(mo + 1) * 128]), queue="sp")
                    psO = ctx.bank()
                    ctx.mm_group(psO, [((wp, wp.ap[:, kt * 128:(kt + 1) * 128]),
                                        (yfb, yfb.ap[:, kt * CH:(kt + 1) * CH]))
                                       for kt in range(8)])
                    aslice = attn_acc.ap[:, mo * CH:(mo + 1) * CH]
                    av = (attn_acc, aslice[:, ::-1] if rev else aslice)
                    if gi == 0:
                        ctx.act("act", av, psO, AF.Copy)
                    else:
                        ctx.tt("dve", av, av, psO, OP.add)
                wp.free()
            out_c = cc
            for dt in range(DT):
                at = P.dtile(cfg["attn_name"], (dt, out_c),
                             attn_h[dt * 128:(dt + 1) * 128, out_c * CH:(out_c + 1) * CH])
                av_ap = attn_acc.ap[:, dt * CH:(dt + 1) * CH]
                if cfg["acc"]:
                    tmp = zres[dt % 2]
                    ctx.dma(tmp, at, queue="sp")
                    ctx.tt("dve", (attn_acc, av_ap), (attn_acc, av_ap), tmp, OP.add)
                ctx.dma(at, (attn_acc, av_ap), queue="sp")
            for z_ in zres:
                z_.free()
            for t_ in (yfb, dt_g, u_g, sp_s, sp_t, a_scr, b_scr, h_scr, bbc, cbc, y_g, ssmB, ssmC, xc):
                t_.free()
            if dtr is not None:
                dtr.free()
            if dtw is not None:
                dtw.free()
    state.free(); xh.free(); attn_acc.free()



def combine_chunk(P, c, xin_name, xout_name, ffn_tile, extra_xn=False):
    ctx = P.ctx
    xin_h = P.dram[xin_name][0]
    xout_h = P.dram[xout_name][0]
    attn_h = P.dram[P.attn_name][0]
    xf = ctx.alloc("cmb_xf", F32, 128, DT * CH)
    af = ctx.alloc("cmb_af", F32, 128, DT * CH)
    xin_r = xin_h.ap().rearrange("(dt p) t -> p dt t", p=128) if hasattr(xin_h, "ap") else None
    xt = P.dtile(xin_name, ("c", c), xin_h.ap().rearrange("(dt p) t -> p dt t", p=128)[:, :, c * CH:(c + 1) * CH])
    ctx.dma((xf, xf.ap.rearrange("p (dt t) -> p dt t", dt=DT)), xt, queue="sp")
    for dt in range(DT):
        at = P.dtile(P.attn_name, (dt, c), attn_h[dt * 128:(dt + 1) * 128, c * CH:(c + 1) * CH])
        ctx.dma((af, af.ap[:, dt * CH:(dt + 1) * CH]), at, queue="sp")
    if extra_xn:
        psM, psR = ctx.bank(), ctx.bank()
        bcast_row(P, psM, (P.m_row, P.m_row.ap[:, c * CH:(c + 1) * CH]))
        bcast_row(P, psR, (P.r_row, P.r_row.ap[:, c * CH:(c + 1) * CH]))
        tmp = ctx.alloc("cmb_tmp", F32, 128, CH)
    for dt in range(DT):
        av = (af, af.ap[:, dt * CH:(dt + 1) * CH])
        xv = (xf, xf.ap[:, dt * CH:(dt + 1) * CH])
        fv = (ffn_tile, ffn_tile.ap[:, dt * CH:(dt + 1) * CH])
        ctx.tt("dve", av, av, fv, OP.add)
        ctx.tt("dve", av, av, xv, OP.add)
        if extra_xn:
            ctx.tt("dve", tmp, xv, (psM, psM.ap[:, :]), OP.subtract)
            ctx.tt("dve", tmp, tmp, (psR, psR.ap[:, :]), OP.mult)
            ctx.tt("dve", av, av, tmp, OP.add)
    ot = P.dtile(xout_name, ("c", c),
                 xout_h.ap().rearrange("(dt p) t -> p dt t", p=128)[:, :, c * CH:(c + 1) * CH])
    ctx.dma(ot, (af, af.ap.rearrange("p (dt t) -> p dt t", dt=DT)), queue="sp")
    if extra_xn:
        tmp.free()
    xf.free()
    af.free()



def emit_block0(P, xin_name, xout_name):
    ctx = P.ctx
    win = P.win
    P.attn_name = "attn0"
    P.dram_scratch("attn0", D, T)
    convw = ctx.alloc("b0_convw", F32, 128, 64)
    convb = ctx.alloc("b0_convb", F32, 128, 16)
    a_neg = ctx.alloc("b0_aneg", F32, 128, DIT * NSTATE)
    d_par = ctx.alloc("b0_d", F32, 128, 16)
    wx = ctx.alloc("b0_wx", BF16, 128, DIT * 33)
    dtb_row = P.rows.alloc("b0_dtb", DI)
    s_cols = {}
    for nm, ncol in [("l0f_s1", 32), ("l0f_s2", 32), ("l0f_hs1", 32), ("l0f_hs2", 32), ("l0f_s3", DT)]:
        s_cols[nm] = ctx.alloc("b0_" + nm, F32, 128, ncol)
    with ctx.phase("b0_res"):
        ctx.dma(convw, win["l0_convw"].ap(), queue="sp")
        ctx.dma(convb, win["l0_convb"].ap(), queue="sp")
        ctx.dma(a_neg, win["l0_alog"].ap(), queue="sp")
        ctx.dma(d_par, win["l0_d"].ap(), queue="sp")
        ctx.dma(dtb_row, win["l0_dtb_row"].ap(), queue="sp")
        ctx.dma((wx, wx.ap.rearrange("p (kt o) -> p kt o", kt=DIT)),
                win["l0_wx"].ap().rearrange("(kt p) o -> p kt o", p=128), queue="pool")
        for nm in s_cols:
            ctx.dma(s_cols[nm], win[nm].ap(), queue="sp")
        ctx.act("act", a_neg, a_neg, AF.Exp)
        ctx.ts("dve", a_neg, a_neg, -1.0, None, OP.mult)
    s_win = stage_bf16(P, win["l0_win"].ap(), "s_l0win", 1024, 4096)
    s_wout = stage_bf16(P, win["l0_wout"].ap(), "s_l0wout", 2048, 1024)
    s_w3 = stage_bf16(P, win["l0f_w3"].ap(), "s_l0w3", 4096, 1024)
    emit_prelude(P, P.dram[xin_name][0], need_rho=False)
    emit_mamba(P, "b0m", dict(
        win_bf=s_win, wout_bf=s_wout, wx=wx, no=33, variant="v1",
        wdt=None, bdt_row=None, dtb_row=dtb_row, convw=convw, convb=convb,
        a_neg=a_neg, d_par=d_par, rho_mul=False, const_row=None,
        rev=False, acc=False, zs_name="zs0", attn_name="attn0"))
    t1_h = P.dram_scratch("l0f_t1", 4096, T, BF16)
    t2_h = P.dram_scratch("l0f_t2", 4096, T, BF16)
    t3_h = P.dram_scratch("l0f_t3", 4096, T, BF16)
    w1_r = win["l0f_w1"].ap().rearrange("(kt p) h -> p kt h", p=128)
    w2_r = win["l0f_w2"].ap().rearrange("(kt p) h -> p kt h", p=128)
    with ctx.phase("b0f_s1"):
        xnres = ctx.alloc("b0f_xn", BF16, 128, DT * T)
        for c in range(NCH):
            xnt = P.dtile("xn", ("c", c), P.xn_h.ap().rearrange(
                "(dt p) t -> p dt t", p=128)[:, :, c * CH:(c + 1) * CH])
            ctx.dma((xnres, xnres.ap.rearrange("p (dt t) -> p dt t", dt=DT)[:, :, c * CH:(c + 1) * CH]),
                    xnt, queue="sp")
        pan1 = [ctx.alloc(f"b0f_p1{i}", BF16, 128, DT * 128) for i in range(2)]
        pan2 = [ctx.alloc(f"b0f_p2{i}", BF16, 128, DT * 128) for i in range(2)]
        f1 = ctx.alloc("b0f_f1", F32, 128, CH)
        stg = [ctx.alloc(f"b0f_st{i}", BF16, 128, T) for i in range(2)]
        for m in range(32):
            p1, p2 = pan1[m % 2], pan2[m % 2]
            st = stg[m % 2]
            ctx.dma((p1, p1.ap.rearrange("p (kt e) -> p kt e", kt=DT)),
                    w1_r[:, :, m * 128:(m + 1) * 128], queue="pool")
            ctx.dma((p2, p2.ap.rearrange("p (kt e) -> p kt e", kt=DT)),
                    w2_r[:, :, m * 128:(m + 1) * 128], queue="pool")
            for c in range(NCH):
                ps1, ps2 = ctx.bank(), ctx.bank()
                ctx.mm_group(ps1, [((p1, p1.ap[:, kt * 128:(kt + 1) * 128]),
                                    (xnres, xnres.ap[:, kt * T + c * CH: kt * T + (c + 1) * CH]))
                                   for kt in range(DT)])
                ctx.mm_group(ps2, [((p2, p2.ap[:, kt * 128:(kt + 1) * 128]),
                                    (xnres, xnres.ap[:, kt * T + c * CH: kt * T + (c + 1) * CH]))
                                   for kt in range(DT)])
                ctx.act("act", f1, ps1, AF.Silu, scale=(s_cols["l0f_s1"], s_cols["l0f_s1"].ap[:, m:m + 1]))
                ctx.stt("dve", (st, st.ap[:, c * CH:(c + 1) * CH]), ps2,
                        (s_cols["l0f_s2"], s_cols["l0f_s2"].ap[:, m:m + 1]), f1, OP.mult, OP.mult)
            t1t = P.dtile("l0f_t1", ("r", m), t1_h[m * 128:(m + 1) * 128, :])
            ctx.dma(t1t, st, queue="sp")
        for x in pan1 + pan2 + stg:
            x.free()
        f1.free()
        xnres.free()
    s_h1 = stage_bf16(P, win["l0f_h1"].ap(), "s_l0h1", 4096, 4096)
    s_h2 = stage_bf16(P, win["l0f_h2"].ap(), "s_l0h2", 4096, 4096)
    for sname, src_h, dst_h, wh, skey in [("s2", t1_h, t2_h, s_h1, "l0f_hs1"),
                                          ("s3", t2_h, t3_h, s_h2, "l0f_hs2")]:
        h_r = wh.handle.ap().rearrange("(kt p) h -> p kt h", p=128)
        TQ = 1024
        for q in range(4):
            cols = slice(q * TQ, (q + 1) * TQ)
            with ctx.phase(f"b0f_{sname}_{q}"):
                tin = ctx.alloc(f"b0f_{sname}_tin", BF16, 128, 32 * TQ)
                srcn = "l0f_t1" if src_h is t1_h else "l0f_t2"
                for kt in range(32):
                    srct = P.dtile(srcn, ("r", kt), src_h[kt * 128:(kt + 1) * 128, :])
                    ctx.dma((tin, tin.ap[:, kt * TQ:(kt + 1) * TQ]),
                            (srct, src_h[kt * 128:(kt + 1) * 128, cols]), queue="sp")
                pan = [ctx.alloc(f"b0f_{sname}p{i}", BF16, 128, 32 * 128) for i in range(2)]
                stg = [ctx.alloc(f"b0f_{sname}st{i}", BF16, 128, TQ) for i in range(2)]
                for m in range(32):
                    p = pan[m % 2]
                    st = stg[m % 2]
                    ctx.dma((p, p.ap.rearrange("p (kt e) -> p kt e", kt=32)),
                            (wh, h_r[:, :, m * 128:(m + 1) * 128]), queue="sp")
                    for sc in range(TQ // CH):
                        ps = ctx.bank()
                        ctx.mm_group(ps, [((p, p.ap[:, kt * 128:(kt + 1) * 128]),
                                           (tin, tin.ap[:, kt * TQ + sc * CH: kt * TQ + (sc + 1) * CH]))
                                          for kt in range(32)])
                        ctx.act("act", (st, st.ap[:, sc * CH:(sc + 1) * CH]), ps, AF.Gelu,
                                scale=(s_cols[skey], s_cols[skey].ap[:, m:m + 1]))
                    dstt = P.dtile("l0f_t2" if dst_h is t2_h else "l0f_t3", ("r", m),
                                   dst_h[m * 128:(m + 1) * 128, :])
                    ctx.dma((dstt, dst_h[m * 128:(m + 1) * 128, cols]), st, queue="sp")
                for x in pan + stg:
                    x.free()
                tin.free()
    w3_r = s_w3.handle.ap().rearrange("(kt p) h -> p kt h", p=128)
    for c in range(NCH):
        with ctx.phase(f"b0f_s4_{c}"):
            tin = ctx.alloc("b0f_s4tin", BF16, 128, 32 * CH)
            for kt in range(32):
                srct = P.dtile("l0f_t3", ("r", kt), t3_h[kt * 128:(kt + 1) * 128, :])
                ctx.dma((tin, tin.ap[:, kt * CH:(kt + 1) * CH]),
                        (srct, t3_h[kt * 128:(kt + 1) * 128, c * CH:(c + 1) * CH]), queue="sp")
            ffn = ctx.alloc("b0f_ffn", F32, 128, DT * CH)
            pan = [ctx.alloc(f"b0f_s4p{i}", BF16, 128, 32 * 128) for i in range(2)]
            for m in range(DT):
                p = pan[m % 2]
                ctx.dma((p, p.ap.rearrange("p (kt e) -> p kt e", kt=32)),
                        (s_w3, w3_r[:, :, m * 128:(m + 1) * 128]), queue="sp")
                ps = ctx.bank()
                ctx.mm_group(ps, [((p, p.ap[:, kt * 128:(kt + 1) * 128]),
                                   (tin, tin.ap[:, kt * CH:(kt + 1) * CH])) for kt in range(32)])
                ctx.act("act", (ffn, ffn.ap[:, m * CH:(m + 1) * CH]), ps, AF.Copy,
                        scale=(s_cols["l0f_s3"], s_cols["l0f_s3"].ap[:, m:m + 1]))
            combine_chunk(P, c, xin_name, xout_name, ffn, extra_xn=False)
            ffn.free()
            tin.free()
            for x in pan:
                x.free()
    for x in [convw, convb, a_neg, d_par, wx] + list(s_cols.values()):
        x.free()
    P.rows.free(dtb_row)



def _const_row_pass(P, tag, wbf_h, b_bf, out_row, C=4096):
    ctx = P.ctx
    nec = C // CH
    with ctx.phase(f"{tag}_const"):
        wt = ctx.alloc(f"{tag}_cw", BF16, 128, C)
        banks = [ctx.bank() for _ in range(nec)]
        w_r = wbf_h.handle.ap()
        for kt in range(DT):
            ctx.dma(wt, (wbf_h, w_r[kt * 128:(kt + 1) * 128, :]), queue="sp")
            for ec in range(nec):
                ctx.mm((banks[ec], banks[ec].ap[0:1, :]),
                       (b_bf, b_bf.ap[:, kt:kt + 1]),
                       (wt, wt.ap[:, ec * CH:(ec + 1) * CH]),
                       start=(kt == 0), stop=(kt == DT - 1), inc=True)
        for ec in range(nec):
            ctx.act("act", (out_row, out_row.ap[0:1, ec * CH:(ec + 1) * CH]),
                    (banks[ec], banks[ec].ap[0:1, :]), AF.Copy)
        wt.free()


def emit_block1(P, xin_name, xout_name):
    ctx = P.ctx
    win = P.win
    P.attn_name = "attn1"
    P.dram_scratch("attn1", D, T)
    emit_prelude(P, P.dram[xin_name][0], need_rho=True)
    g_col = ctx.alloc("b1_g", F32, 128, DT)
    b_colbf = ctx.alloc("b1_b", BF16, 128, DT)
    with ctx.phase("b1_gres"):
        ctx.dma(g_col, win["l1_g"].ap(), queue="sp")
        ctx.dma(b_colbf, win["l1_b"].ap(), queue="pool")
    for pfx in ("f", "b"):
        rev = (pfx == "b")
        convw = ctx.alloc(f"b1{pfx}_convw", F32, 128, 64)
        convb = ctx.alloc(f"b1{pfx}_convb", F32, 128, 16)
        a_neg = ctx.alloc(f"b1{pfx}_aneg", F32, 128, DIT * NSTATE)
        d_par = ctx.alloc(f"b1{pfx}_d", F32, 128, 16)
        wx = ctx.alloc(f"b1{pfx}_wx", BF16, 128, DIT * 96)
        wdt = ctx.alloc(f"b1{pfx}_wdt", BF16, 64, DIT * 128)
        bdt_row = P.rows.alloc(f"b1{pfx}_bdt", DI)
        const_row = P.rows.alloc(f"b1{pfx}_const", 2 * DI)
        with ctx.phase(f"b1{pfx}_res"):
            ctx.dma(convw, win[f"l1{pfx}_convw"].ap(), queue="sp")
            ctx.dma(convb, win[f"l1{pfx}_convb"].ap(), queue="sp")
            ctx.dma(a_neg, win[f"l1{pfx}_alog"].ap(), queue="sp")
            ctx.dma(d_par, win[f"l1{pfx}_d"].ap(), queue="sp")
            ctx.dma(bdt_row, win[f"l1{pfx}_bdt"].ap(), queue="sp")
            ctx.dma((wx, wx.ap.rearrange("p (kt o) -> p kt o", kt=DIT)),
                    win[f"l1{pfx}_wx"].ap().rearrange("(kt p) o -> p kt o", p=128), queue="pool")
            ctx.dma((wdt, wdt.ap[0:64, :].rearrange("p (kt o) -> p kt o", kt=DIT)),
                    win[f"l1{pfx}_wdt"].ap().rearrange("r (kt o) -> r kt o", kt=DIT), queue="pool")
            ctx.act("act", a_neg, a_neg, AF.Exp)
            ctx.ts("dve", a_neg, a_neg, -1.0, None, OP.mult)
        s_win = stage_bf16(P, win[f"l1{pfx}_win"].ap(), f"s_l1{pfx}win", 1024, 4096,
                           scale_col=(g_col, lambda kt: g_col.ap[:, kt:kt + 1]))
        s_wout = stage_bf16(P, win[f"l1{pfx}_wout"].ap(), f"s_l1{pfx}wout", 2048, 1024)
        _const_row_pass(P, f"b1{pfx}", s_win, b_colbf, const_row)
        emit_mamba(P, f"b1{pfx}", dict(
            win_bf=s_win, wout_bf=s_wout, wx=wx, no=96, variant="v2",
            wdt=wdt, bdt_row=bdt_row, dtb_row=None, convw=convw, convb=convb,
            a_neg=a_neg, d_par=d_par, rho_mul=True, const_row=const_row,
            rev=rev, acc=rev, zs_name=f"zs1{pfx}", attn_name="attn1"))
        for x in (convw, convb, a_neg, d_par, wx, wdt):
            x.free()
        P.rows.free(const_row)
        P.rows.free(bdt_row)
    emit_moe(P, xin_name, xout_name)
    g_col.free(); b_colbf.free()


def emit_moe(P, xin_name, xout_name):
    ctx = P.ctx
    win = P.win
    E = 8
    wg = ctx.alloc("moe_wg", BF16, 128, DT * E)
    b1c = ctx.alloc("moe_b1", F32, 128, E * 32)
    b2bf = ctx.alloc("moe_b2", BF16, 8, D)
    gam = ctx.alloc("moe_gam", F32, 128, DT)
    bet = ctx.alloc("moe_bet", F32, 128, DT)
    bg_row = P.rows.alloc("moe_bg", 8)
    ctm = ctx.alloc("moe_ctm", F32, 128, 32 * E)
    cfm = ctx.alloc("moe_cfm", BF16, 8, T)
    xnc_g = ctx.alloc("moe_xnc", BF16, 128, DT * CH)
    with ctx.phase("moe_res"):
        ctx.dma((wg, wg.ap.rearrange("p (kt e) -> p kt e", kt=DT)),
                win["l1f_wg"].ap().rearrange("(kt p) e -> p kt e", p=128), queue="pool")
        ctx.dma((b1c, b1c.ap.rearrange("p (e c) -> p e c", e=E)),
                win["l1f_b1"].ap().rearrange("e p c -> p e c"), queue="sp")
        ctx.dma(b2bf, win["l1f_b2"].ap(), queue="pool")
        ctx.dma(gam, win["l1f_gam"].ap(), queue="sp")
        ctx.dma(bet, win["l1f_bet"].ap(), queue="sp")
        ctx.dma(bg_row, win["l1f_bg"].ap(), queue="sp")
    with ctx.phase("moe_gate"):
        lg = ctx.alloc("moe_lg", F32, 8, CH)
        lgt = ctx.alloc("moe_lgt", F32, 128, 4 * E)
        m1 = ctx.alloc("moe_m1", F32, 128, E)
        m2 = ctx.alloc("moe_m2", F32, 128, E)
        x2 = ctx.alloc("moe_x2", F32, 128, E)
        gsc = ctx.alloc("moe_gsc", F32, 128, 4)
        for c in range(NCH):
            xnt = P.dtile("xn", ("c", c), P.xn_h.ap().rearrange(
                "(dt p) t -> p dt t", p=128)[:, :, c * CH:(c + 1) * CH])
            ctx.dma((xnc_g, xnc_g.ap.rearrange("p (dt t) -> p dt t", dt=DT)), xnt, queue="sp")
            psL = ctx.bank()
            bgv = (bg_row, bg_row.ap[0:1, 0:8])
            ctx.mm_group((psL, psL.ap[0:8, :]),
                         [((wg, wg.ap[:, kt * E:(kt + 1) * E]),
                           (xnc_g, xnc_g.ap[:, kt * CH:(kt + 1) * CH]))
                          for kt in range(DT)],
                         extra=[(bgv, ones_rhs(P, bgv))])
            ctx.act("act", (lg, lg.ap[0:8, :]), (psL, psL.ap[0:8, :]), AF.Copy)
            for tt_ in range(4):
                psT = ctx.bank()
                ctx.transpose((psT, psT.ap[:, 0:8]), (lg, lg.ap[0:8, tt_ * 128:(tt_ + 1) * 128]),
                              (P.g["ident_f"], P.g["ident_f"].ap[0:8, 0:8]))
                ctx.act("act", (lgt, lgt.ap[:, tt_ * E:(tt_ + 1) * E]), (psT, psT.ap[:, 0:8]), AF.Copy)
            for tt_ in range(4):
                lv = (lgt, lgt.ap[:, tt_ * E:(tt_ + 1) * E])
                v1 = (gsc, gsc.ap[:, 0:1]); v2 = (gsc, gsc.ap[:, 1:2])
                s1 = (gsc, gsc.ap[:, 2:3]); s2 = (gsc, gsc.ap[:, 3:4])
                ctx.red("dve", v1, lv, OP.max)
                ctx.ts("dve", m1, lv, v1, None, OP.is_ge)
                ctx.stt("dve", x2, m1, -1e30, lv, OP.mult, OP.add)
                ctx.red("dve", v2, x2, OP.max)
                ctx.tt("dve", s1, v1, v2, OP.subtract)
                ctx.act("act", s1, s1, AF.Sigmoid)
                ctx.ts("dve", s2, s1, -1.0, 1.0, OP.mult, OP.add)
                ctx.ts("dve", m2, lv, v2, None, OP.is_ge)
                ctx.tt("dve", m2, m2, m1, OP.subtract)
                ctx.ts("dve", m1, m1, s1, None, OP.mult)
                ctx.ts("dve", m2, m2, s2, None, OP.mult)
                gt = c * 4 + tt_
                ctx.tt("dve", (ctm, ctm.ap[:, gt * E:(gt + 1) * E]), m1, m2, OP.add)
                psT = ctx.bank()
                ctx.transpose((psT, psT.ap[0:8, 0:128]),
                              (ctm, ctm.ap[:, gt * E:(gt + 1) * E]), P.g["ident_f"])
                ctx.act("act", (cfm, cfm.ap[0:8, gt * 128:(gt + 1) * 128]),
                        (psT, psT.ap[0:8, 0:128]), AF.Copy)
        for x in (lg, lgt, m1, m2, x2, gsc):
            x.free()
        xnc_g.free()
    HB = 256
    NHB = 4096 // HB
    w1_h = win["l1f_w1"]
    w2_h = win["l1f_w2"]
    acc = ctx.alloc("moe_acc", F32, 128, 16 * D)
    scr = ctx.alloc("moe_scr", F32, 128, D)
    scr2 = ctx.alloc("moe_scr2", F32, 128, D)
    sbf = ctx.alloc("moe_sbf", BF16, 128, D)
    col4 = ctx.alloc("moe_col4", F32, 128, 4)
    for half in range(2):
        with ctx.phase(f"moe_e_{half}"):
            xnh = ctx.alloc("moe_xnh", BF16, 128, DT * 2048)
            pan1 = [ctx.alloc(f"moe_p1{i}", BF16, 128, DT * HB) for i in range(2)]
            pan2 = [ctx.alloc(f"moe_p2{i}", BF16, 128, (HB // 128) * D) for i in range(2)]
            hbf = ctx.alloc("moe_h", BF16, 128, (HB // 128) * CH)
            for sc in range(4):
                c_abs = half * 4 + sc
                xnt = P.dtile("xn", ("c", c_abs), P.xn_h.ap().rearrange(
                    "(dt p) t -> p dt t", p=128)[:, :, c_abs * CH:(c_abs + 1) * CH])
                ctx.dma((xnh, xnh.ap.rearrange("p (dt t) -> p dt t", dt=DT)[:, :, sc * CH:(sc + 1) * CH]),
                        xnt, queue="sp")
            for sc in range(4):
                for ts_ in range(4):
                    for dt in range(DT):
                        psT = ctx.bank()
                        pbf = psT.ap.bitcast(BF16)
                        ctx.transpose((psT, pbf[:, 0:128]),
                                      (xnh, xnh.ap[:, dt * 2048 + sc * CH + ts_ * 128: dt * 2048 + sc * CH + (ts_ + 1) * 128]),
                                      P.g["ident_b"])
                        ctx.act("act", (acc, acc.ap[:, (sc * 4 + ts_) * D + dt * 128: (sc * 4 + ts_) * D + (dt + 1) * 128]),
                                (psT, pbf[:, 0:128]), AF.Copy)
            pi = 0
            for e in range(E):
                for hb in range(NHB):
                    p1 = pan1[pi % 2]
                    p2 = pan2[pi % 2]
                    pi += 1
                    ctx.dma((p1, p1.ap.rearrange("p (kt h) -> p kt h", kt=DT)),
                            w1_h.ap()[e].rearrange("(kt p) h -> p kt h", p=128)[:, :, hb * HB:(hb + 1) * HB],
                            queue="pool")
                    ctx.dma((p2, p2.ap.rearrange("p (kt d) -> p kt d", kt=HB // 128)),
                            w2_h.ap()[e].rearrange("(kt p) d -> p kt d", p=128)[:, hb * (HB // 128):(hb + 1) * (HB // 128), :],
                            queue="pool")
                    for sc in range(4):
                        c_abs = half * 4 + sc
                        for mh in range(HB // 128):
                            psH = ctx.bank()
                            ctx.mm_group(psH, [((p1, p1.ap[:, kt * HB + mh * 128: kt * HB + (mh + 1) * 128]),
                                                (xnh, xnh.ap[:, kt * 2048 + sc * CH: kt * 2048 + (sc + 1) * CH]))
                                               for kt in range(DT)])
                            ctx.act("act", (hbf, hbf.ap[:, mh * CH:(mh + 1) * CH]), psH, AF.Gelu,
                                    bias=(b1c, b1c.ap[:, e * 32 + hb * (HB // 128) + mh: e * 32 + hb * (HB // 128) + mh + 1]))
                        for ts_ in range(4):
                            gt = c_abs * 4 + ts_
                            for dh in range(2):
                                psO = ctx.bank()
                                pairs = [((hbf, hbf.ap[:, mh * CH + ts_ * 128: mh * CH + ts_ * 128 + 128]),
                                          (p2, p2.ap[:, mh * D + dh * CH: mh * D + (dh + 1) * CH]))
                                         for mh in range(HB // 128)]
                                extra = None
                                if hb == 0:
                                    extra = [((cfm, cfm.ap[0:8, gt * 128:(gt + 1) * 128]),
                                              (b2bf, b2bf.ap[0:8, dh * CH:(dh + 1) * CH]))]
                                ctx.mm_group(psO, pairs, extra=extra)
                                av = (acc, acc.ap[:, (sc * 4 + ts_) * D + dh * CH: (sc * 4 + ts_) * D + (dh + 1) * CH])
                                ctx.stt("dve", av, psO, (ctm, ctm.ap[:, gt * E + e: gt * E + e + 1]),
                                        av, OP.mult, OP.add)
        xnh.free()
        for x in pan1 + pan2 + [hbf]:
            x.free()
        ffn = ctx.alloc("moe_ffn", F32, 128, DT * CH)
        for sc in range(4):
            c_abs = half * 4 + sc
            with ctx.phase(f"moe_ln_{c_abs}"):
                for ts_ in range(4):
                    av = (acc, acc.ap[:, (sc * 4 + ts_) * D: (sc * 4 + ts_ + 1) * D])
                    vsum = (col4, col4.ap[:, 0:1]); negmu = (col4, col4.ap[:, 1:2])
                    vcol = (col4, col4.ap[:, 2:3]); rstd = (col4, col4.ap[:, 3:4])
                    ctx.red("dve", vsum, av, OP.add)
                    ctx.ts("dve", negmu, vsum, -1.0 / D, None, OP.mult)
                    ctx.ts("dve", scr, av, negmu, None, OP.add)
                    ctx.act("act", scr2, scr, AF.Square, accum_out=vcol)
                    ctx.ts("dve", vcol, vcol, 1.0 / D, EPS, OP.mult, OP.add)
                    ctx.act("act", vcol, vcol, AF.Ln)
                    ctx.act("act", rstd, vcol, AF.Exp, scale=-0.5)
                    ctx.ts("dve", sbf, scr, rstd, None, OP.mult)
                    for dt in range(DT):
                        psT = ctx.bank()
                        pbf = psT.ap.bitcast(BF16)
                        ctx.transpose((psT, pbf[:, 0:128]),
                                      (sbf, sbf.ap[:, dt * 128:(dt + 1) * 128]), P.g["ident_b"])
                        ctx.ts("dve", (ffn, ffn.ap[:, dt * CH + ts_ * 128: dt * CH + (ts_ + 1) * 128]),
                               (psT, pbf[:, 0:128]), (gam, gam.ap[:, dt:dt + 1]),
                               (bet, bet.ap[:, dt:dt + 1]), OP.mult, OP.add)
                combine_chunk(P, c_abs, xin_name, xout_name, ffn, extra_xn=True)
        ffn.free()
    for x in (wg, b1c, b2bf, gam, bet, ctm, cfm, acc, scr, scr2, sbf, col4):
        x.free()


def emit_block2(P, xin_name, xout_name):
    ctx = P.ctx
    win = P.win
    g = P.g
    P.attn_name = "attn2"
    P.dram_scratch("attn2", D, T)
    emit_prelude(P, P.dram[xin_name][0], need_rho=True)
    lng = ctx.alloc("b2_lng", F32, 128, DT)
    lnb_bf = ctx.alloc("b2_lnbb", BF16, 128, DT)
    lsg = ctx.alloc("b2_lsg", F32, 128, DT)
    lr = ctx.alloc("b2_lr", F32, 128, DT)
    with ctx.phase("b2_res"):
        ctx.dma(lng, win["l2_lng"].ap(), queue="sp")
        ctx.dma(lnb_bf, win["l2_lnb"].ap(), queue="pool")
        ctx.dma(lsg, win["l2_lsg"].ap(), queue="sp")
        ctx.dma(lr, win["l2_loglr"].ap(), queue="sp")
        ctx.act("act", lr, lr, AF.Exp)
        ctx.ts("dve", lr, lr, 1.0, None, OP.min)
    sc_ = (lng, lambda kt: lng.ap[:, kt:kt + 1])
    s_wq = stage_bf16(P, win["l2_wq"].ap(), "s_l2wq", D, D, scale_col=sc_)
    s_wk = stage_bf16(P, win["l2_wk"].ap(), "s_l2wk", D, D, scale_col=sc_)
    s_wv = stage_bf16(P, win["l2_wv"].ap(), "s_l2wv", D, D, scale_col=sc_)
    s_wg = stage_bf16(P, win["l2_wg"].ap(), "s_l2wg", D, D)
    s_wo = stage_bf16(P, win["l2_wo"].ap(), "s_l2wo", D, D)
    s_w0 = stage_bf16(P, win["l2_w0"].ap(), "s_l2w0", D, D)
    cq = P.rows.alloc("b2_cq", D)
    ck = P.rows.alloc("b2_ck", D)
    cv = P.rows.alloc("b2_cv", D)
    _const_row_pass(P, "b2q", s_wq, lnb_bf, cq, C=D)
    _const_row_pass(P, "b2k", s_wk, lnb_bf, ck, C=D)
    _const_row_pass(P, "b2v", s_wv, lnb_bf, cv, C=D)
    q_h = P.dram_scratch("l2_q", D, T)
    k_h = P.dram_scratch("l2_k", D, T)
    kb_h = P.dram_scratch("l2_kb", D, T, BF16)
    v_h = P.dram_scratch("l2_v", D, T)

    def wres(name, h):
        t = ctx.alloc(name, BF16, 128, DT * D)
        ctx.dma((t, t.ap.rearrange("p (kt m) -> p kt m", kt=DT)),
                (h, h.handle.ap().rearrange("(kt p) m -> p kt m", p=128)), queue="sp")
        return t

    with ctx.phase("b2_qkv"):
        wqr, wkr, wvr = wres("b2_wqr", s_wq), wres("b2_wkr", s_wk), wres("b2_wvr", s_wv)
        xnc = ctx.alloc("b2_xnc", BF16, 128, DT * CH)
        xnr = ctx.alloc("b2_xnr", BF16, 128, DT * CH)
        st1 = ctx.alloc("b2_st1", F32, 128, CH)
        st2 = ctx.alloc("b2_st2", BF16, 128, CH)
        for c in range(NCH):
            xnt = P.dtile("xn", ("c", c), P.xn_h.ap().rearrange(
                "(dt p) t -> p dt t", p=128)[:, :, c * CH:(c + 1) * CH])
            ctx.dma((xnc, xnc.ap.rearrange("p (dt t) -> p dt t", dt=DT)), xnt, queue="sp")
            psR = ctx.bank()
            bcast_row(P, psR, (P.rho_row, P.rho_row.ap[:, c * CH:(c + 1) * CH]))
            for dt in range(DT):
                ctx.tt("dve", (xnr, xnr.ap[:, dt * CH:(dt + 1) * CH]),
                       (xnc, xnc.ap[:, dt * CH:(dt + 1) * CH]), (psR, psR.ap[:, :]), OP.mult)
            for wt, crow, outh, outbf in [(wqr, cq, q_h, None), (wkr, ck, k_h, kb_h),
                                          (wvr, cv, v_h, None)]:
                for m in range(DT):
                    ps = ctx.bank()
                    crv = (crow, crow.ap[0:1, m * 128:(m + 1) * 128])
                    ctx.mm_group(ps, [((wt, wt.ap[:, kt * D + m * 128: kt * D + (m + 1) * 128]),
                                       (xnr, xnr.ap[:, kt * CH:(kt + 1) * CH])) for kt in range(DT)],
                                 extra=[(crv, ones_rhs(P, crv))])
                    ctx.act("act", st1, ps, AF.Copy)
                    key = {id(q_h): "l2_q", id(k_h): "l2_k", id(v_h): "l2_v"}[id(outh)]
                    dtl = P.dtile(key, (m, c), outh[m * 128:(m + 1) * 128, c * CH:(c + 1) * CH])
                    ctx.dma(dtl, st1, queue="sp")
                    if outbf is not None:
                        ctx.copy("dve", st2, st1)
                        dtb = P.dtile("l2_kb", (m, c), outbf[m * 128:(m + 1) * 128, c * CH:(c + 1) * CH])
                        ctx.dma(dtb, st2, queue="sp")
        for x in (wqr, wkr, wvr, xnc, xnr, st1, st2):
            x.free()

    with ctx.phase("b2_main"):
        w0r = wres("b2_w0r", s_w0)
        wgr = wres("b2_wgr", s_wg)
        wor = wres("b2_wor", s_wo)
        ab = ctx.alloc("b2_ab", BF16, 128, DT * 8)
        bib = ctx.alloc("b2_bib", BF16, 8, D)
        ctx.dma((ab, ab.ap.rearrange("p (kt r) -> p kt r", kt=DT)),
                win["l2_ainit"].ap().rearrange("(kt p) r -> p kt r", p=128), queue="pool")
        ctx.dma(bib, win["l2_binit"].ap(), queue="pool")
        carry = ctx.alloc("b2_carry", F32, 128, DT)
        ctx.memset("dve", carry, 0.0)
        kbf = ctx.alloc("b2_kbf", BF16, 128, DT * CH)
        xnc = ctx.alloc("b2_xnc2", BF16, 128, DT * CH)
        kab = ctx.alloc("b2_kab", BF16, 8, CH)
        vt = ctx.alloc("b2_vt", F32, 128, CH)
        kt_ = ctx.alloc("b2_kt", F32, 128, CH)
        qt = ctx.alloc("b2_qt", F32, 128, CH)
        e_s = ctx.alloc("b2_es", F32, 128, CH)
        g_s = ctx.alloc("b2_gs", F32, 128, CH)
        gsb = ctx.alloc("b2_gsb", F32, 128, CH + 1)
        c2 = ctx.alloc("b2_c2", F32, 128, CH)
        gte = ctx.alloc("b2_gte", F32, 128, CH)
        obf = ctx.alloc("b2_obf", BF16, 128, DT * CH)
        asc = ctx.alloc("b2_asc", F32, 128, DT * CH)
        prow = P.rows.alloc("b2_prow", CH)
        for c in range(NCH):
            xnt = P.dtile("xn", ("c", c), P.xn_h.ap().rearrange(
                "(dt p) t -> p dt t", p=128)[:, :, c * CH:(c + 1) * CH])
            ctx.dma((xnc, xnc.ap.rearrange("p (dt t) -> p dt t", dt=DT)), xnt, queue="sp")
            for m in range(DT):
                kbt = P.dtile("l2_kb", (m, c), kb_h[m * 128:(m + 1) * 128, c * CH:(c + 1) * CH])
                ctx.dma((kbf, kbf.ap[:, m * CH:(m + 1) * CH]), kbt, queue="sp")
            ctx.iota(ctx.iotmp, [[1, 512]], cm=0)
            ctx.act("act", prow, (ctx.iotmp, ctx.iotmp.ap[0:1, 0:CH]), AF.Ln, bias=float(c * CH + 1))
            ctx.ts("dve", prow, prow, 0.1, 1.0, OP.mult, OP.add)
            ctx.recip(prow, prow)
            psP = ctx.bank()
            bcast_row(P, psP, prow)
            ctx.act("act", (asc, asc.ap[:, 0:CH]), psP, AF.Copy)
            psA = ctx.bank()
            ctx.mm_group((psA, psA.ap[0:8, :]),
                         [((ab, ab.ap[:, kt * 8:(kt + 1) * 8]),
                           (kbf, kbf.ap[:, kt * CH:(kt + 1) * CH])) for kt in range(DT)])
            ctx.act("act", (kab, kab.ap[0:8, :]), (psA, psA.ap[0:8, :]), AF.Copy)
            for m in range(DT):
                for nm, hh, dst in (("l2_v", v_h, vt), ("l2_k", k_h, kt_), ("l2_q", q_h, qt)):
                    dtl = P.dtile(nm, (m, c), hh[m * 128:(m + 1) * 128, c * CH:(c + 1) * CH])
                    ctx.dma(dst, dtl, queue="sp")
                psY = ctx.bank()
                biv = (bib, bib.ap[0:8, m * 128:(m + 1) * 128])
                ctx.mm_group(psY, [((w0r, w0r.ap[:, kt * D + m * 128: kt * D + (m + 1) * 128]),
                                    (kbf, kbf.ap[:, kt * CH:(kt + 1) * CH])) for kt in range(DT)],
                             extra=[((bib, biv[1]), (kab, kab.ap[0:8, :]))])
                ctx.tt("dve", e_s, psY, vt, OP.subtract)
                ctx.tt("dve", g_s, kt_, e_s, OP.mult)
                ctx.copy("dve", (gsb, gsb.ap[:, 0:1]), (carry, carry.ap[:, m:m + 1]))
                ctx.scan("dve", (gsb, gsb.ap[:, 1:CH + 1]), (g["ones_row_f"], g["ones_row_f"].ap),
                         g_s, (carry, carry.ap[:, m:m + 1]))
                ctx.copy("dve", (carry, carry.ap[:, m:m + 1]), (gsb, gsb.ap[:, CH:CH + 1]))
                ctx.stt("dve", c2, qt, (lr, lr.ap[:, m:m + 1]), (gsb, gsb.ap[:, 0:CH]),
                        OP.mult, OP.mult)
                ctx.tt("dve", c2, c2, (asc, asc.ap[:, 0:CH]), OP.mult)
                ctx.tt("dve", e_s, e_s, c2, OP.subtract)
                ctx.tt("dve", e_s, e_s, vt, OP.add)
                psG = ctx.bank()
                ctx.mm_group(psG, [((wgr, wgr.ap[:, kt * D + m * 128: kt * D + (m + 1) * 128]),
                                    (xnc, xnc.ap[:, kt * CH:(kt + 1) * CH])) for kt in range(DT)])
                ctx.act("act", gte, psG, AF.Sigmoid)
                ctx.tt("dve", (obf, obf.ap[:, m * CH:(m + 1) * CH]), e_s, gte, OP.mult)
            for mo in range(DT):
                psO = ctx.bank()
                ctx.mm_group(psO, [((wor, wor.ap[:, kt * D + mo * 128: kt * D + (mo + 1) * 128]),
                                    (obf, obf.ap[:, kt * CH:(kt + 1) * CH])) for kt in range(DT)])
                ctx.ts("dve", (asc, asc.ap[:, mo * CH:(mo + 1) * CH]), psO,
                       (lsg, lsg.ap[:, mo:mo + 1]), None, OP.mult)
            for dt in range(DT):
                at = P.dtile("attn2", (dt, c), P.dram["attn2"][0][dt * 128:(dt + 1) * 128, c * CH:(c + 1) * CH])
                ctx.dma(at, (asc, asc.ap[:, dt * CH:(dt + 1) * CH]), queue="sp")
        for x in (w0r, wgr, wor, ab, bib, carry, kbf, xnc, kab, vt, kt_, qt, e_s, g_s,
                  gsb, c2, gte, obf, asc):
            x.free()
        P.rows.free(prow)

    b_row = P.rows.alloc("b2_brow", D)
    with ctx.phase("b2_lora"):
        ctx.dma(b_row, win["l2f_b"].ap(), queue="sp")
        lupT = ctx.alloc("b2_lupT", BF16, 4, 64)
        ldwn = ctx.alloc("b2_ldwn", BF16, 4, 128)
        uaxT = ctx.alloc("b2_uaxT", BF16, 50, D)
        daux = ctx.alloc("b2_daux", BF16, 100, D)
        c1b = ctx.alloc("b2_c1b", BF16, 50, 128)
        c2b = ctx.alloc("b2_c2b", BF16, 128, DT * 128)
        c2t = ctx.alloc("b2_c2t", BF16, 100, DT * 128)
        wlin = ctx.alloc("b2_wlin", BF16, 128, DT * D)
        wtf = ctx.alloc("b2_wtf", F32, 128, D)
        ctx.dma((lupT, lupT.ap[0:4, 0:50]), win["l2f_lupT"].ap(), queue="pool")
        ctx.dma((ldwn, ldwn.ap[0:4, 0:100]), win["l2f_ldown"].ap(), queue="pool")
        ctx.dma((uaxT, uaxT.ap[0:50, :]), win["l2f_uauxT"].ap(), queue="pool")
        ctx.dma((daux, daux.ap[0:100, :]), win["l2f_daux"].ap(), queue="pool")
        ps1 = ctx.bank()
        ctx.mm((ps1, ps1.ap[0:50, 0:100]), (lupT, lupT.ap[0:4, 0:50]),
               (ldwn, ldwn.ap[0:4, 0:100]), start=True, stop=True, inc=True)
        ctx.act("act", (c1b, c1b.ap[0:50, 0:100]), (ps1, ps1.ap[0:50, 0:100]), AF.Copy)
        for mp in range(DT):
            ps2 = ctx.bank()
            ctx.mm((ps2, ps2.ap[:, 0:100]), (uaxT, uaxT.ap[0:50, mp * 128:(mp + 1) * 128]),
                   (c1b, c1b.ap[0:50, 0:100]), start=True, stop=True, inc=True)
            ctx.act("act", (c2b, c2b.ap[:, mp * 128: mp * 128 + 100]), (ps2, ps2.ap[:, 0:100]), AF.Copy)
            psT = ctx.bank()
            pbf = psT.ap.bitcast(BF16)
            ctx.transpose((psT, pbf[0:100, 0:128]),
                          (c2b, c2b.ap[:, mp * 128: mp * 128 + 100]), P.g["ident_b"])
            ctx.act("act", (c2t, c2t.ap[0:100, mp * 128:(mp + 1) * 128]), (psT, pbf[0:100, 0:128]), AF.Copy)
        for kt in range(DT):
            psL0, psL1 = ctx.bank(), ctx.bank()
            ctx.mm((psL0, psL0.ap[:, :]), (daux, daux.ap[0:100, kt * 128:(kt + 1) * 128]),
                   (c2t, c2t.ap[0:100, 0:CH]), start=True, stop=True, inc=True)
            ctx.mm((psL1, psL1.ap[:, :]), (daux, daux.ap[0:100, kt * 128:(kt + 1) * 128]),
                   (c2t, c2t.ap[0:100, CH:2 * CH]), start=True, stop=True, inc=True)
            ctx.dma(wtf, win["l2f_w"].ap()[kt * 128:(kt + 1) * 128, :], queue="sp")
            ctx.tt("dve", (wlin, wlin.ap[:, kt * D: kt * D + CH]), psL0, (wtf, wtf.ap[:, 0:CH]), OP.add)
            ctx.tt("dve", (wlin, wlin.ap[:, kt * D + CH: (kt + 1) * D]), psL1,
                   (wtf, wtf.ap[:, CH:2 * CH]), OP.add)
        for x in (lupT, ldwn, uaxT, daux, c1b, c2b, c2t, wtf):
            x.free()
    for c in range(NCH):
        with ctx.phase(f"b2_ffn_{c}"):
            xnc = ctx.alloc("b2_xnc3", BF16, 128, DT * CH)
            ffn = ctx.alloc("b2_ffn", F32, 128, DT * CH)
            xnt = P.dtile("xn", ("c", c), P.xn_h.ap().rearrange(
                "(dt p) t -> p dt t", p=128)[:, :, c * CH:(c + 1) * CH])
            ctx.dma((xnc, xnc.ap.rearrange("p (dt t) -> p dt t", dt=DT)), xnt, queue="sp")
            for m in range(DT):
                ps = ctx.bank()
                brv = (b_row, b_row.ap[0:1, m * 128:(m + 1) * 128])
                ctx.mm_group(ps, [((wlin, wlin.ap[:, kt * D + m * 128: kt * D + (m + 1) * 128]),
                                   (xnc, xnc.ap[:, kt * CH:(kt + 1) * CH])) for kt in range(DT)],
                             extra=[(brv, ones_rhs(P, brv))])
                ctx.act("act", (ffn, ffn.ap[:, m * CH:(m + 1) * CH]), ps, AF.Copy)
            combine_chunk(P, c, xin_name, xout_name, ffn, extra_xn=True)
            xnc.free()
            ffn.free()
    wlin.free()
    P.rows.free(b_row)
    for x in (lng, lnb_bf, lsg, lr):
        x.free()
    P.rows.free(cv); P.rows.free(ck); P.rows.free(cq)


def emit_block3(P, xin_name, xout_name):
    ctx = P.ctx
    g = P.g
    win = P.win
    P.attn_name = "attn3"
    P.dram_scratch("attn3", D, T)
    emit_prelude(P, P.dram[xin_name][0], need_rho=False)
    s_qkv = stage_bf16(P, win["l3_wqkv"].ap(), "s_l3qkv", D, 3 * D)
    s_wo3 = stage_bf16(P, win["l3_wout"].ap(), "s_l3wo", D, D)
    proj = ctx.alloc("b3_proj", BF16, 128, 64)
    bo_row = P.rows.alloc("b3_bo", D)
    with ctx.phase("b3_res"):
        ctx.dma(proj, win["l3_proj"].ap(), queue="pool")
        ctx.dma(bo_row, win["l3_bout"].ap(), queue="sp")
    qp_h = P.dram_scratch("l3_qp", 512, T)
    kp_h = P.dram_scratch("l3_kp", 512, T)
    v3_h = P.dram_scratch("l3_v", D, T)
    WS = 7
    for hh in range(2):
        with ctx.phase(f"b3_a{hh}"):
            pans = ctx.alloc("b3_pans", BF16, 128, DT * 12 * 128)
            qkv_r = s_qkv.handle.ap().rearrange("(kt p) m -> p kt m", p=128)
            for j, base in enumerate((hh * 512, D + hh * 512, 2 * D + hh * 512)):
                ctx.dma((pans, pans.ap.rearrange("p (j kt m) -> p j kt m", j=3, kt=DT)[:, j:j + 1].squeeze(1)),
                        (s_qkv, qkv_r[:, :, base:base + 512]), queue="sp")
            xnc = ctx.alloc("b3_xnc", BF16, 128, DT * CH)
            qbf = ctx.alloc("b3_qbf", BF16, 128, 4 * CH)
            kbf2 = ctx.alloc("b3_kbf", BF16, 128, 4 * CH)
            st1 = ctx.alloc("b3_st1", F32, 128, CH)
            phs = ctx.alloc("b3_phs", F32, 64, 3 * CH)
            for c in range(NCH):
                xnt = P.dtile("xn", ("c", c), P.xn_h.ap().rearrange(
                    "(dt p) t -> p dt t", p=128)[:, :, c * CH:(c + 1) * CH])
                ctx.dma((xnc, xnc.ap.rearrange("p (dt t) -> p dt t", dt=DT)), xnt, queue="sp")
                for j in range(3):
                    for m in range(4):
                        ps = ctx.bank()
                        ctx.mm_group(ps, [((pans, pans.ap[:, (j * DT + kt) * 512 + m * 128:
                                                          (j * DT + kt) * 512 + (m + 1) * 128]),
                                           (xnc, xnc.ap[:, kt * CH:(kt + 1) * CH])) for kt in range(DT)])
                        if j == 2:
                            ctx.act("act", st1, ps, AF.Copy)
                            vt = P.dtile("l3_v", (hh, m, c),
                                         v3_h[hh * 512 + m * 128: hh * 512 + (m + 1) * 128,
                                              c * CH:(c + 1) * CH])
                            ctx.dma(vt, st1, queue="sp")
                        else:
                            dst = qbf if j == 0 else kbf2
                            ctx.act("act", (dst, dst.ap[:, m * CH:(m + 1) * CH]), ps, AF.Copy)
                for j in range(2):
                    src = qbf if j == 0 else kbf2
                    outh = qp_h if j == 0 else kp_h
                    nm = "l3_qp" if j == 0 else "l3_kp"
                    for h in range(4):
                        psp = ctx.bank()
                        ctx.mm((psp, psp.ap[0:64, :]), (proj, proj.ap[:, 0:64]),
                               (src, src.ap[:, h * CH:(h + 1) * CH]), start=True, stop=True, inc=True)
                        ctx.ts("dve", (phs, phs.ap[0:64, 0:CH]), (psp, psp.ap[0:64, :]),
                               0.0, None, OP.min)
                        ctx.act("act", (phs, phs.ap[0:64, CH:2 * CH]), (phs, phs.ap[0:64, 0:CH]), AF.Exp)
                        ctx.ts("dve", (phs, phs.ap[0:64, 0:CH]), (psp, psp.ap[0:64, :]),
                               0.0, None, OP.max)
                        ctx.tt("dve", (phs, phs.ap[0:64, 2 * CH:3 * CH]),
                               (phs, phs.ap[0:64, 0:CH]), (phs, phs.ap[0:64, CH:2 * CH]), OP.add)
                        pt = P.dtile(nm, (hh, h, c),
                                     outh[(hh * 4 + h) * 64:(hh * 4 + h + 1) * 64, c * CH:(c + 1) * CH])
                        ctx.dma(pt, (phs, phs.ap[0:64, 2 * CH:3 * CH]), queue="sp")
            for x in (pans, xnc, qbf, kbf2, st1, phs):
                x.free()
        with ctx.phase(f"b3_b{hh}"):
            wor = ctx.alloc("b3_wor", BF16, 128, 4 * D)
            wo_r = s_wo3.handle.ap().rearrange("(kt p) m -> p kt m", p=128)
            ctx.dma((wor, wor.ap.rearrange("p (kt m) -> p kt m", kt=4)),
                    (s_wo3, wo_r[:, hh * 4:(hh + 1) * 4, :]), queue="sp")
            qpw = ctx.alloc("b3_qpw", F32, 128, 2 * CH)
            kpw = ctx.alloc("b3_kpw", F32, 128, 2 * 518)
            vw = ctx.alloc("b3_vw", F32, 128, 4 * 518)
            prod = ctx.alloc("b3_prod", F32, 128, CH)
            s_sb = ctx.alloc("b3_ssb", F32, 8, CH)
            zrow = P.rows.alloc("b3_z", CH)
            swb = ctx.alloc("b3_swb", F32, 128, CH)
            acc4 = ctx.alloc("b3_acc4", F32, 128, CH)
            obf3 = ctx.alloc("b3_obf", BF16, 128, 4 * CH)
            att = ctx.alloc("b3_att", F32, 128, DT * CH)
            for c in range(NCH):
                lo = c * CH - 3
                hi = c * CH + CH + 3
                llo = max(lo, 0); lhi = min(hi, T)
                off = llo - lo
                wdt_ = lhi - llo
                kpw3 = kpw.ap.rearrange("p (rt t) -> p rt t", rt=2)
                vw3 = vw.ap.rearrange("p (rt t) -> p rt t", rt=4)
                if off > 0:
                    ctx.memset("dve", (kpw, kpw3[:, :, 0:off]), 0.0)
                    ctx.memset("dve", (vw, vw3[:, :, 0:off]), 0.0)
                if hi > T:
                    e_ = 518 - (hi - T)
                    ctx.memset("dve", (kpw, kpw3[:, :, e_:518]), 0.0)
                    ctx.memset("dve", (vw, vw3[:, :, e_:518]), 0.0)
                cdep = [cc for cc in (c - 1, c, c + 1) if 0 <= cc < NCH]
                kdeps = [P.dtile("l3_kp", (hh, h, cc), None) for h in range(4) for cc in cdep]
                vdeps = [P.dtile("l3_v", (hh, m, cc), None) for m in range(4) for cc in cdep]
                qdeps = [P.dtile("l3_qp", (hh, h, c), None) for h in range(4)]
                ka = kp_h[hh * 256:(hh + 1) * 256, llo:lhi].rearrange("(rt p) t -> p rt t", p=128)
                va = v3_h[hh * 512:(hh + 1) * 512, llo:lhi].rearrange("(rt p) t -> p rt t", p=128)
                qa = qp_h[hh * 256:(hh + 1) * 256, c * CH:(c + 1) * CH].rearrange("(rt p) t -> p rt t", p=128)
                ctx.emit("sp", lambda e, _o=kpw3[:, :, off:off + wdt_], _i=ka: e.dma_start(_o, _i),
                         reads=kdeps, writes=[kpw], dmaq="dsp")
                ctx.emit("sp", lambda e, _o=vw3[:, :, off:off + wdt_], _i=va: e.dma_start(_o, _i),
                         reads=vdeps, writes=[vw], dmaq="dsp")
                ctx.emit("sp", lambda e, _o=qpw.ap.rearrange("p (rt t) -> p rt t", rt=2), _i=qa: e.dma_start(_o, _i),
                         reads=qdeps, writes=[qpw], dmaq="dsp")
                for h in range(4):
                    rt = h // 2
                    bs = (h % 2) * 64
                    psS = ctx.bank()
                    for w in range(WS):
                        ctx.tt("dve", (prod, prod.ap[bs:bs + 64, :]),
                               (qpw, qpw.ap[bs:bs + 64, rt * CH:(rt + 1) * CH]),
                               (kpw, kpw.ap[bs:bs + 64, rt * 518 + w: rt * 518 + w + 512]),
                               OP.mult)
                        ctx.mm((psS, psS.ap[0:8, :]),
                               (g["selW"], g["selW"].ap[bs:bs + 64, w * 8:(w + 1) * 8]),
                               (prod, prod.ap[bs:bs + 64, :]),
                               start=(w == 0), stop=(w == WS - 1), inc=True)
                    ctx.act("act", (s_sb, s_sb.ap[0:8, :]), (psS, psS.ap[0:8, :]), AF.Copy)
                    psZ = ctx.bank()
                    ctx.mm((psZ, psZ.ap[0:1, :]), (g["ones_col_f"], g["ones_col_f"].ap[0:8, 0:1]),
                           (s_sb, s_sb.ap[0:8, :]), start=True, stop=True, inc=True)
                    ctx.act("act", zrow, (psZ, psZ.ap[0:1, :]), AF.Copy)
                    ctx.ts("dve", zrow, zrow, 1e-8, None, OP.add)
                    ctx.recip(zrow, zrow)
                    first = True
                    for w in range(WS):
                        psB = ctx.bank()
                        ctx.mm(psB, (g["sel16"], g["sel16"].ap[0:8, w * 128:(w + 1) * 128]),
                               (s_sb, s_sb.ap[0:8, :]), start=True, stop=True, inc=True)
                        ctx.act("act", swb, psB, AF.Copy)
                        vsl = vw.ap[:, h * 518 + w: h * 518 + w + 512]
                        if first:
                            ctx.tt("dve", acc4, (vw, vsl), swb, OP.mult)
                            first = False
                        else:
                            ctx.tt("dve", prod, (vw, vsl), swb, OP.mult)
                            ctx.tt("dve", acc4, acc4, prod, OP.add)
                    psZ2 = ctx.bank()
                    bcast_row(P, psZ2, zrow)
                    ctx.tt("dve", (obf3, obf3.ap[:, h * CH:(h + 1) * CH]), acc4, (psZ2, psZ2.ap[:, :]), OP.mult)
                for mo in range(DT):
                    psO = ctx.bank()
                    bov = (bo_row, bo_row.ap[0:1, mo * 128:(mo + 1) * 128])
                    ctx.mm_group(psO, [((wor, wor.ap[:, kt * D + mo * 128: kt * D + (mo + 1) * 128]),
                                        (obf3, obf3.ap[:, kt * CH:(kt + 1) * CH])) for kt in range(4)],
                                 extra=[(bov, ones_rhs(P, bov))] if hh == 0 else None)
                    av = (att, att.ap[:, mo * CH:(mo + 1) * CH])
                    at = P.dtile("attn3", (mo, c), P.dram["attn3"][0][mo * 128:(mo + 1) * 128, c * CH:(c + 1) * CH])
                    if hh == 0:
                        ctx.act("act", av, psO, AF.Copy)
                    else:
                        ctx.dma(av, at, queue="sp")
                        ctx.tt("dve", av, av, psO, OP.add)
                    ctx.dma(at, av, queue="sp")
            for x in (wor, qpw, kpw, vw, prod, s_sb, swb, acc4, obf3, att):
                x.free()
            P.rows.free(zrow)
    emit_bitnet(P, xin_name, xout_name)
    proj.free()
    P.rows.free(bo_row)


def emit_bitnet(P, xin_name, xout_name):
    ctx = P.ctx
    win = P.win
    g = P.g
    h_h = P.dram_scratch("l3f_h", 4096, T, BF16)
    b1_row = P.rows.alloc("b3f_b1", 4096)
    b2_row = P.rows.alloc("b3f_b2", D)

    def gamma_pass(tag, src_ap, R, C, qcol):
        with ctx.phase(tag):
            wtf = [ctx.alloc(f"{tag}_w{i}", F32, 128, 4096) for i in range(2)]
            gac = ctx.alloc(f"{tag}_g", F32, 128, 2)
            nk = R // 128
            for kt in range(nk):
                f = wtf[kt % 2]
                ctx.dma((f, f.ap[:, 0:C]), src_ap[kt * 128:(kt + 1) * 128, :], queue="sp")
                ctx.red("dve", (gac, gac.ap[:, 0:1]), (f, f.ap[:, 0:C]), OP.add, absval=True)
                if kt == 0:
                    ctx.copy("dve", (gac, gac.ap[:, 1:2]), (gac, gac.ap[:, 0:1]))
                else:
                    ctx.tt("dve", (gac, gac.ap[:, 1:2]), (gac, gac.ap[:, 1:2]),
                           (gac, gac.ap[:, 0:1]), OP.add)
            psG = ctx.bank()
            ctx.mm((psG, psG.ap[0:1, 0:1]), g["ones_col_f"], (gac, gac.ap[:, 1:2]),
                   start=True, stop=True, inc=True)
            ctx.act("act", (gac, gac.ap[0:1, 0:1]), (psG, psG.ap[0:1, 0:1]), AF.Copy,
                    scale=1.0 / (R * C))
            ctx.ts("dve", (gac, gac.ap[0:1, 0:1]), (gac, gac.ap[0:1, 0:1]), EPS, None, OP.add)
            ctx.recip((gac, gac.ap[0:1, 0:1]), (gac, gac.ap[0:1, 0:1]))
            psB = ctx.bank()
            ctx.mm((psB, psB.ap[:, 0:1]), (g["ones128_f"], g["ones128_f"].ap[0:1, :]),
                   (gac, gac.ap[0:1, 0:1]), start=True, stop=True, inc=True)
            ctx.act("act", qcol, (psB, psB.ap[:, 0:1]), AF.Copy)
            for x in wtf:
                x.free()
            gac.free()

    def quant_tiles(dst_view, src_view, qcol, scr):
        ctx.ts("dve", scr, src_view, qcol, None, OP.mult)
        ctx.ts("dve", dst_view, scr, 0.5, None, OP.is_ge)
        ctx.ts("dve", scr, scr, -0.5, None, OP.is_le)
        ctx.tt("dve", dst_view, dst_view, scr, OP.subtract)

    q1col = ctx.alloc("b3f_q1", F32, 128, 1)
    q2col = ctx.alloc("b3f_q2", F32, 128, 1)
    gamma_pass("b3f_g1", win["l3f_w1"].ap(), 1024, 4096, q1col)
    with ctx.phase("b3f_res"):
        ctx.dma(b1_row, win["l3f_b1"].ap(), queue="sp")
        ctx.dma(b2_row, win["l3f_b2"].ap(), queue="sp")
    with ctx.phase("b3f_mm1"):
        xnres = ctx.alloc("b3f_xn", BF16, 128, DT * T)
        for c in range(NCH):
            xnt = P.dtile("xn", ("c", c), P.xn_h.ap().rearrange(
                "(dt p) t -> p dt t", p=128)[:, :, c * CH:(c + 1) * CH])
            ctx.dma((xnres, xnres.ap.rearrange("p (dt t) -> p dt t", dt=DT)[:, :, c * CH:(c + 1) * CH]),
                    xnt, queue="sp")
        wp = [ctx.alloc(f"b3f_wp{i}", F32, 128, DT * 128) for i in range(2)]
        wq = ctx.alloc("b3f_wq", BF16, 128, DT * 128)
        scr = ctx.alloc("b3f_scr", F32, 128, 128)
        hst = ctx.alloc("b3f_hst", BF16, 128, T)
        w1_r = win["l3f_w1"].ap().rearrange("(kt p) h -> p kt h", p=128)
        for mh in range(32):
            f = wp[mh % 2]
            ctx.dma((f, f.ap.rearrange("p (kt e) -> p kt e", kt=DT)),
                    w1_r[:, :, mh * 128:(mh + 1) * 128], queue="sp")
            for kt in range(DT):
                quant_tiles((wq, wq.ap[:, kt * 128:(kt + 1) * 128]),
                            (f, f.ap[:, kt * 128:(kt + 1) * 128]), q1col, scr)
            for c in range(NCH):
                ps = ctx.bank()
                b1v = (b1_row, b1_row.ap[0:1, mh * 128:(mh + 1) * 128])
                ctx.mm_group(ps, [((wq, wq.ap[:, kt * 128:(kt + 1) * 128]),
                                   (xnres, xnres.ap[:, kt * T + c * CH: kt * T + (c + 1) * CH]))
                                  for kt in range(DT)],
                             extra=[(b1v, ones_rhs(P, b1v))])
                ctx.act("act", (hst, hst.ap[:, c * CH:(c + 1) * CH]), ps, AF.Relu)
            ht = P.dtile("l3f_h", ("r", mh), h_h[mh * 128:(mh + 1) * 128, :])
            ctx.dma(ht, hst, queue="sp")
        for x in wp + [wq, scr, hst, xnres]:
            x.free()
    gamma_pass("b3f_g2", win["l3f_w2"].ap(), 4096, 1024, q2col)
    wq2 = ctx.alloc("b3f_wq2", BF16, 128, 32 * D)
    with ctx.phase("b3f_qw2"):
        wp = [ctx.alloc(f"b3f_w2p{i}", F32, 128, D) for i in range(2)]
        scr = ctx.alloc("b3f_scr2", F32, 128, D)
        w2_r = win["l3f_w2"].ap()
        for kt in range(32):
            f = wp[kt % 2]
            ctx.dma(f, w2_r[kt * 128:(kt + 1) * 128, :], queue="sp")
            quant_tiles((wq2, wq2.ap[:, kt * D:(kt + 1) * D]), f, q2col, scr)
        for x in wp + [scr]:
            x.free()
    for c in range(NCH):
        with ctx.phase(f"b3f_mm2_{c}"):
            hin = ctx.alloc("b3f_hin", BF16, 128, 32 * CH)
            ffn = ctx.alloc("b3f_ffn", F32, 128, DT * CH)
            for kt in range(32):
                ht = P.dtile("l3f_h", ("r", kt), h_h[kt * 128:(kt + 1) * 128, :])
                ctx.dma((hin, hin.ap[:, kt * CH:(kt + 1) * CH]),
                        (ht, h_h[kt * 128:(kt + 1) * 128, c * CH:(c + 1) * CH]), queue="sp")
            for mo in range(DT):
                ps = ctx.bank()
                b2v = (b2_row, b2_row.ap[0:1, mo * 128:(mo + 1) * 128])
                ctx.mm_group(ps, [((wq2, wq2.ap[:, kt * D + mo * 128: kt * D + (mo + 1) * 128]),
                                   (hin, hin.ap[:, kt * CH:(kt + 1) * CH])) for kt in range(32)],
                             extra=[(b2v, ones_rhs(P, b2v))])
                ctx.act("act", (ffn, ffn.ap[:, mo * CH:(mo + 1) * CH]), ps, AF.Copy)
            combine_chunk(P, c, xin_name, xout_name, ffn, extra_xn=False)
            hin.free()
            ffn.free()
    wq2.free(); q1col.free(); q2col.free()
    P.rows.free(b2_row); P.rows.free(b1_row)



ARENA_B = 200 * 1024

def build_program(wspecs, blocks=(0, 1, 2, 3), dbg=()):
    nc = bass.Bass("TRN2", target_bir_lowering=False)
    es = contextlib.ExitStack()
    nc._es = es
    win = {}
    for name, shape in wspecs.items():
        win[name] = nc.dram_tensor("w_" + name, list(shape), F32, kind="ExternalInput")
    x_in = nc.dram_tensor("x_fm", [D, T], F32, kind="ExternalInput")
    x_out = nc.dram_tensor("xout", [D, T], F32, kind="ExternalOutput")
    arena = es.enter_context(nc.sbuf_tensor("arena", [128, ARENA_B], U8))
    iotmp = es.enter_context(nc.sbuf_tensor("iotmp", [128, 512], F32))
    psum = [es.enter_context(nc.psum_tensor(f"psb{i}", [128, CH], F32)) for i in range(8)]
    sems = [es.enter_context(nc.semaphore(f"s{i}")) for i in range(74)]
    ctx = Ctx(nc, sems, arena, ARENA_B, psum)
    ctx.iotmp = Tile("iotmp", iotmp[:, :])
    P = Prog(nc, ctx, win, set(dbg))
    P.dram["x_fm"] = (x_in, {})
    P.dram["xout"] = (x_out, {})
    P.xn_h = P.dram_scratch("xn", D, T, BF16)
    P.rows = RowPool(ctx)
    P.m_row = P.rows.alloc("m_row", T)
    P.r_row = P.rows.alloc("r_row", T)
    P.rho_row = P.rows.alloc("rho_row", T)
    emit_setup(P)
    emitters = {0: emit_block0, 1: emit_block1, 2: emit_block2, 3: emit_block3}
    blocks = list(blocks)
    names = ["x_fm"] + [f"xi{b}" for b in blocks[1:]] + ["xout"]
    for i, b in enumerate(blocks):
        xin, xout = names[i], names[i + 1]
        if xout != "xout":
            P.dram_scratch(xout, D, T)
        P.xin_name = xin
        emitters[b](P, xin, xout)
    with ctx.phase("fin"):
        ctx.drain_all_dma()
    return nc


def kernel(**inputs):
    x = np.asarray(inputs["L0_attn_x"], dtype=np.float32)
    w = prepare_weights(inputs)
    wspecs = {k: list(v.shape) for k, v in w.items()}
    nc = build_program(wspecs, blocks=(0, 1, 2, 3))
    in_maps = []
    for core in range(8):
        m = {"w_" + k: v for k, v in w.items()}
        m["x_fm"] = np.ascontiguousarray(x[core].T)
        in_maps.append(m)
    res = run_bass_kernel_spmd(nc, in_maps, core_ids=list(range(8)))
    out = np.stack([np.ascontiguousarray(r["xout"].T) for r in res.results])
    return out.astype(np.float32)
